# version 03 (final)
# baseline (speedup 1.0000x reference)
import math
import numpy as np
import ml_dtypes
import concourse.bass as bass
import concourse.mybir as mybir
from concourse.bass_utils import run_bass_kernel_spmd
from contextlib import ExitStack

F32 = mybir.dt.float32
BF16 = mybir.dt.bfloat16
AF = mybir.ActivationFunctionType
ALU = mybir.AluOpType
AX = mybir.AxisListType
BF = ml_dtypes.bfloat16

D = 1024
T = 2048
BLOC = 2
NTOK = BLOC * T
NCORES = 8
DT = D // 128
NH, HD = 16, 64
NPAIR = 8
LAM = 1e-4
QK_SCALE = 1.0 / (math.sqrt(1024.0) + 1e-8)
ENGINES = ("sync", "tensor", "vector", "scalar", "gpsimd")
NLAYERS = 4
L3DBG = False
L1_UPTO = "all"


class Sem:
    def __init__(self, handle):
        self.h = handle
        self.count = 0

    def tick(self, n=1):
        self.count += n
        return self.count


class B:
    def __init__(self, nc, ctx):
        self.nc = nc
        self.ctx = ctx
        self.ops = {e: [] for e in ENGINES}
        self._pool = []
        self._active = []
        self._sid = 0
        self._bar = self.sem("barrier")
        self.hb = {}

    def alloc(self, name, shape, dtype):
        return self.ctx.enter_context(self.nc.sbuf_tensor(name, shape, dtype))

    def psum(self, name, shape, dtype=F32):
        return self.ctx.enter_context(self.nc.psum_tensor(name, shape, dtype))

    def sem(self, name=None):
        if self._pool:
            sm = self._pool.pop()
        else:
            self._sid += 1
            h = self.ctx.enter_context(self.nc.semaphore(f"s{self._sid}"))
            sm = Sem(h)
        self._active.append(sm)
        return sm

    def sem_mark(self):
        return len(self._active)

    def sem_release(self, mark):
        self._pool.extend(self._active[mark:])
        del self._active[mark:]

    def wait(self, eng, sem, val=None):
        v = sem.count if val is None else val
        if v > 0:
            self.ops[eng].append(lambda e, s=sem.h, v=v: e.wait_ge(s, v))

    def op(self, eng, fn, inc=None, n=1):
        if inc is not None:
            inc.tick(n)
            self.ops[eng].append(
                lambda e, f=fn, s=inc.h, n=n: f(e).then_inc(s, n))
        else:
            self.ops[eng].append(fn)

    def dma(self, eng, out, in_, sem=None, **kw):
        if sem is not None:
            sem.tick(16)
            self.ops[eng].append(
                lambda e, o=out, i=in_, s=sem.h, kw=kw:
                e.dma_start(out=o, in_=i, **kw).then_inc(s, 16))
        else:
            self.ops[eng].append(
                lambda e, o=out, i=in_, kw=kw: e.dma_start(out=o, in_=i, **kw))

    def drain(self, eng):
        self.ops[eng].append(lambda e: e.drain())

    def barrier(self, engines=ENGINES):
        k = len(engines)
        tgt = self._bar.count + k
        self._bar.count = tgt
        for e in engines:
            self.ops[e].append(lambda eng, s=self._bar.h: eng.sem_inc(s, 1))
            self.ops[e].append(lambda eng, s=self._bar.h, t=tgt: eng.wait_ge(s, t))

    def emit(self):
        block = self.ctx.enter_context(self.nc.Block())
        for eng in ENGINES:
            fns = self.ops[eng]
            getattr(block, eng)(lambda engine, fns=fns: [f(engine) for f in fns])



def make_constants():
    c = {}
    c["ident_bf"] = np.eye(128, dtype=np.float32).astype(BF)
    c["ident_f"] = np.eye(128, dtype=np.float32)
    c["ones_col_bf"] = np.ones((128, 1), dtype=BF)
    c["ones_row_bf"] = np.ones((1, 128), dtype=BF)
    c["ones_row_f"] = np.ones((1, 128), dtype=np.float32)
    t = np.arange(T, dtype=np.float64)
    u = np.exp(-LAM * (t - 1024.0))
    c["u_bc"] = np.tile(u.astype(np.float32), (128, 1))
    c["uinv_bc"] = np.tile((1.0 / u).astype(np.float32), (128, 1))
    w = np.exp(LAM * (t - 1024.0))
    c["wdec_col"] = np.ascontiguousarray(w.astype(np.float32).reshape(16, 128).T)
    c["wdecinv_col"] = np.ascontiguousarray(
        (1.0 / w).astype(np.float32).reshape(16, 128).T)
    p = np.arange(128)
    dif = p[None, :] - p[:, None]
    c["dprime"] = np.where(dif >= 0, 1.0, np.exp(2 * LAM * dif)).astype(BF)
    c["mdiag"] = (dif >= 0).astype(BF)
    c["mband"] = (dif <= 0).astype(BF)
    d64 = np.arange(64)
    invf = (1.0 / (10000.0 ** ((2 * (d64 // 2)) / 64.0)))[None, :]
    pos = np.arange(T, dtype=np.float64)[:, None]
    fr = pos * invf
    cs = np.cos(fr).T.astype(np.float32)
    sn = np.sin(fr).T.astype(np.float32)
    c["cos_tab"] = np.concatenate([cs, cs], 0)
    c["sin_tab"] = np.concatenate([sn, sn], 0)
    r2t = np.zeros((128, 128), dtype=np.float32)
    for blk in range(2):
        for i in range(32):
            base = blk * 64 + 2 * i
            r2t[base + 1, base] = -1.0
            r2t[base, base + 1] = 1.0
    c["r2t"] = r2t.astype(BF)
    i64 = np.arange(64)
    tri = (i64[:, None] <= i64[None, :])
    c["tri64"] = np.concatenate([tri, tri], 0).astype(BF)
    i128 = np.arange(128)
    c["tri128s"] = (i128[:, None] < i128[None, :]).astype(BF)
    c["ones512"] = np.ones((1, 512), dtype=BF)
    return c

CONST_DTYPES = {"ident_bf": BF16, "ident_f": F32, "ones_col_bf": BF16, "ones_row_bf": BF16,
                "ones_row_f": F32, "u_bc": F32, "uinv_bc": F32, "wdec_col": F32,
                "wdecinv_col": F32, "dprime": BF16, "mdiag": BF16, "mband": BF16,
                "cos_tab": F32, "sin_tab": F32, "r2t": BF16,
                "tri64": BF16, "tri128s": BF16, "ones512": BF16}


def stage_core_inputs(inputs, core):
    b0 = core * BLOC
    m = {}
    f32 = lambda k: np.ascontiguousarray(np.asarray(inputs[k], dtype=np.float32))
    x = np.asarray(inputs["L0_attn_x"])[b0:b0 + BLOC]
    m["x0"] = np.ascontiguousarray(
        np.asarray(x, dtype=np.float32).transpose(2, 0, 1).reshape(D, NTOK))
    m["l0_wq"] = f32("L0_attn_Wq")
    m["l0_wk"] = f32("L0_attn_Wk")
    m["l0_wv"] = f32("L0_attn_Wv")
    m["l0_wo"] = f32("L0_attn_Wo")
    m["l0_wdown"] = f32("L0_attn_W_down_kv")
    m["l0_wupk"] = f32("L0_attn_W_up_k")
    wupv = f32("L0_attn_W_up_v")
    aug = np.zeros((256, NPAIR * 130), dtype=np.float32)
    for t_ in range(NPAIR):
        aug[:, t_ * 130:t_ * 130 + 64] = wupv[:, (2 * t_) * 64:(2 * t_ + 1) * 64]
        aug[:, t_ * 130 + 65:t_ * 130 + 129] = wupv[:, (2 * t_ + 1) * 64:(2 * t_ + 2) * 64]
    m["l0_wupv_aug"] = aug
    onesrow = np.zeros((1, NPAIR * 130), dtype=np.float32)
    onesrow[0, 64::130] = 1.0
    onesrow[0, 129::130] = 1.0
    m["l0_vaug_ones"] = onesrow
    m["l0_gate"] = f32("L0_ffn_gate_kernel")
    m["l1_wqT"] = np.ascontiguousarray(f32("L1_attn_Wq").T)
    m["l1_wkT"] = np.ascontiguousarray(f32("L1_attn_Wk").T)
    m["l1_woT"] = np.ascontiguousarray(f32("L1_attn_Wo").T)
    wvT = np.ascontiguousarray(f32("L1_attn_Wv").T)
    aug1 = np.zeros((D, NPAIR * 130), dtype=np.float32)
    for t_ in range(NPAIR):
        aug1[:, t_ * 130:t_ * 130 + 64] = wvT[:, (2 * t_) * 64:(2 * t_ + 1) * 64]
        aug1[:, t_ * 130 + 65:t_ * 130 + 129] = wvT[:, (2 * t_ + 1) * 64:(2 * t_ + 2) * 64]
    m["l1_wvT_aug"] = aug1
    sqk = f32("L1_attn_s_qk")
    m["l1_sqk"] = np.ascontiguousarray(
        sqk.reshape(-1)[None, :].reshape(8, 128).T)
    m["l1_wg"] = f32("L1_ffn_w_gate")
    wqkv = f32("L2_attn_w_qkv")
    m["l2_wqkv"] = wqkv
    bqkv = f32("L2_attn_b_qkv")
    m["l2_bqkv"] = np.ascontiguousarray(bqkv[0:2048].reshape(16, 128).T)
    m["l2_wout"] = f32("L2_attn_w_out")
    m["l2_bout"] = np.ascontiguousarray(f32("L2_attn_b_out").reshape(8, 128).T)
    aug2 = np.zeros((D, NPAIR * 130), dtype=np.float32)
    vrow2 = np.zeros((1, NPAIR * 130), dtype=np.float32)
    for t_ in range(NPAIR):
        aug2[:, t_ * 130:t_ * 130 + 64] = wqkv[:, 2048 + (2 * t_) * 64: 2048 + (2 * t_ + 1) * 64]
        aug2[:, t_ * 130 + 65:t_ * 130 + 129] = wqkv[:, 2048 + (2 * t_ + 1) * 64: 2048 + (2 * t_ + 2) * 64]
        vrow2[0, t_ * 130 + 64] = 1.0
        vrow2[0, t_ * 130 + 129] = 1.0
        vrow2[0, t_ * 130:t_ * 130 + 64] = bqkv[2048 + (2 * t_) * 64: 2048 + (2 * t_ + 1) * 64]
        vrow2[0, t_ * 130 + 65:t_ * 130 + 129] = bqkv[2048 + (2 * t_ + 1) * 64: 2048 + (2 * t_ + 2) * 64]
    m["l2_wv_aug"] = aug2
    m["l2_vrow"] = vrow2
    m["l3_wshT"] = np.ascontiguousarray(f32("L3_attn_W_shift").T)
    m["l3_sgate"] = np.ascontiguousarray(f32("L3_attn_shift_gate").reshape(8, 128).T)
    m["l3_lnw"] = np.ascontiguousarray(f32("L3_attn_ln_w").reshape(8, 128).T)
    m["l3_lnb"] = np.ascontiguousarray(f32("L3_attn_ln_b").reshape(8, 128).T)
    m["l3_wkvT"] = np.ascontiguousarray(
        np.concatenate([f32("L3_attn_W_key").T, f32("L3_attn_W_value").T], axis=1))
    m["l3_td"] = np.ascontiguousarray(f32("L3_attn_time_decay")[None, :])
    m["l3_tf"] = np.ascontiguousarray(f32("L3_attn_time_first")[None, :])
    m["l3_woutT"] = np.ascontiguousarray(f32("L3_attn_W_output").T)
    m["l3_rt"] = np.ascontiguousarray(f32("L3_ffn_routing_token").reshape(8, 128).T)
    m["l3_lg"] = np.ascontiguousarray(f32("L3_ffn_light_gamma").reshape(8, 128).T)
    m["l3_hg"] = np.ascontiguousarray(f32("L3_ffn_heavy_gamma").reshape(8, 128).T)
    m["l3_lw1"] = f32("L3_ffn_light_w1")
    m["l3_lb1"] = np.ascontiguousarray(f32("L3_ffn_light_b1").reshape(4, 128).T)
    m["l3_lw2"] = f32("L3_ffn_light_w2")
    m["l3_lb2row"] = np.ascontiguousarray(f32("L3_ffn_light_b2")[None, :])
    m["l3_hw1"] = f32("L3_ffn_heavy_w1")
    m["l3_hb1"] = np.ascontiguousarray(f32("L3_ffn_heavy_b1").reshape(32, 128).T)
    m["l3_hw2"] = f32("L3_ffn_heavy_w2")
    m["l3_hb2row"] = np.ascontiguousarray(f32("L3_ffn_heavy_b2")[None, :])
    m["l2_wrT"] = np.ascontiguousarray(f32("L2_ffn_W_r").T)
    m["l2_wkT"] = np.ascontiguousarray(f32("L2_ffn_W_k").T)
    m["l2_wvT"] = np.ascontiguousarray(f32("L2_ffn_W_v").T)
    m["l2_mur"] = np.ascontiguousarray(f32("L2_ffn_mu_r").reshape(8, 128).T)
    m["l2_muk"] = np.ascontiguousarray(f32("L2_ffn_mu_k").reshape(8, 128).T)
    m["l1_wu"] = f32("L1_ffn_w_up")
    m["l1_wd"] = f32("L1_ffn_w_down")
    m["l0_w0"] = f32("L0_ffn_w0_kernel")
    m["l0_w1"] = f32("L0_ffn_w1_kernel")
    m["l0_woe"] = f32("L0_ffn_wo_kernel")
    return m

INPUT_SPECS_L1 = {
    "l1_wqT": ([D, D], F32), "l1_wkT": ([D, D], F32), "l1_woT": ([D, D], F32),
    "l1_wvT_aug": ([D, NPAIR * 130], F32), "l1_sqk": ([128, 8], F32),
    "l1_wg": ([2816, D], F32), "l1_wu": ([2816, D], F32),
    "l1_wd": ([D, 2816], F32),
}

INPUT_SPECS_L2 = {
    "l2_wqkv": ([D, 3 * D], F32), "l2_bqkv": ([128, 16], F32),
    "l2_wout": ([D, D], F32), "l2_bout": ([128, 8], F32),
    "l2_wv_aug": ([D, NPAIR * 130], F32), "l2_vrow": ([1, NPAIR * 130], F32),
    "l2_wrT": ([D, D], F32), "l2_wkT": ([D, D], F32), "l2_wvT": ([D, D], F32),
    "l2_mur": ([128, 8], F32), "l2_muk": ([128, 8], F32),
}

INPUT_SPECS_L3 = {
    "l3_wshT": ([D, D], F32), "l3_sgate": ([128, 8], F32),
    "l3_lnw": ([128, 8], F32), "l3_lnb": ([128, 8], F32),
    "l3_wkvT": ([D, 128], F32), "l3_td": ([1, 64], F32), "l3_tf": ([1, 64], F32),
    "l3_woutT": ([64, D], F32), "l3_rt": ([128, 8], F32),
    "l3_lg": ([128, 8], F32), "l3_hg": ([128, 8], F32),
    "l3_lw1": ([D, 512], F32), "l3_lb1": ([128, 4], F32),
    "l3_lw2": ([512, D], F32), "l3_lb2row": ([1, D], F32),
    "l3_hw1": ([D, 4096], F32), "l3_hb1": ([128, 32], F32),
    "l3_hw2": ([4096, D], F32), "l3_hb2row": ([1, D], F32),
}

INPUT_SPECS_L0 = {
    "x0": ([D, NTOK], F32),
    "l0_wq": ([D, D], F32), "l0_wk": ([D, D], F32), "l0_wv": ([D, D], F32),
    "l0_wo": ([D, D], F32), "l0_wdown": ([D, 256], F32),
    "l0_wupk": ([256, D], F32), "l0_wupv_aug": ([256, NPAIR * 130], F32),
    "l0_vaug_ones": ([1, NPAIR * 130], F32),
    "l0_gate": ([D, 8], F32),
    "l0_w0": ([8, D, 2048], F32), "l0_w1": ([8, D, 2048], F32),
    "l0_woe": ([8, 2048, D], F32),
}



def build_ln(b, xn, x_hbm, pfx):
    nc, hb = b.nc, b.hb
    _sm = b.sem_mark()
    ld = b.sem(pfx + "ld")
    NCH = NTOK // 512
    with ExitStack() as sc:
        E = sc.enter_context
        xb = E(nc.sbuf_tensor(pfx + "xb", [128, DT * NTOK], BF16))
        x2 = E(nc.sbuf_tensor(pfx + "x2", [128, 2 * 512], BF16))
        rows = E(nc.sbuf_tensor(pfx + "rows", [1, 4 * 512], F32))
        abrow = E(nc.sbuf_tensor(pfx + "abrow", [1, 2 * 512], BF16))
        abb = E(nc.sbuf_tensor(pfx + "abb", [128, 2 * NTOK], BF16))
        tmp = E(nc.sbuf_tensor(pfx + "tmp", [128, 512], BF16))
        ps1 = E(nc.psum_tensor(pfx + "ps1", [1, 512], F32))
        ps2 = E(nc.psum_tensor(pfx + "ps2", [1, 512], F32))
        psb = [E(nc.psum_tensor(pfx + f"psb{i}", [128, 512], F32)) for i in range(2)]

        for dt in range(DT):
            b.dma("gpsimd", xb[:, dt * NTOK:(dt + 1) * NTOK],
                  x_hbm[dt * 128:dt * 128 + 128, :], ld)
        vs = b.sem(pfx + "v")
        mm = b.sem(pfx + "mm")
        ev = b.sem(pfx + "ev")
        mm0 = mm.count
        ssem = b.sem(pfx + "sc")
        vr = b.sem(pfx + "vr")
        bmm = b.sem(pfx + "bmm")
        bev = b.sem(pfx + "bev")
        b.wait("vector", ld)
        b.wait("tensor", ld)
        step = 0
        for ch in range(NCH):
            for dt in range(DT):
                xv = xb[:, dt * NTOK + ch * 512: dt * NTOK + ch * 512 + 512]
                x2v = x2[:, (step % 2) * 512:(step % 2) * 512 + 512]
                if step >= 2:
                    b.wait("vector", mm, mm0 + step - 1)
                b.op("vector", lambda v, xv=xv, x2v=x2v: v.tensor_mul(x2v, xv, xv),
                     inc=vs)
                b.wait("tensor", vs)
                if dt == 0:
                    b.wait("tensor", ev)
                b.op("tensor", lambda t, xv=xv, dt=dt: t.matmul(
                    ps1[:, :], hb["ones_col_bf"][:, 0:1], xv,
                    start=(dt == 0), stop=(dt == DT - 1)))
                b.op("tensor", lambda t, x2v=x2v, dt=dt: t.matmul(
                    ps2[:, :], hb["ones_col_bf"][:, 0:1], x2v,
                    start=(dt == 0), stop=(dt == DT - 1)), inc=mm)
                step += 1
            s1 = rows[:, 0:512]
            s2 = rows[:, 512:1024]
            sc1 = rows[:, 1024:1536]
            sc2 = rows[:, 1536:2048]
            b.wait("vector", mm)
            b.wait("vector", ssem)
            b.op("vector", lambda v: v.tensor_copy(s1, ps1[:, :]))
            b.op("vector", lambda v: v.tensor_copy(s2, ps2[:, :]), inc=ev)
            b.drain("vector")
            b.op("vector", lambda v: v.tensor_scalar(
                out=sc1, in0=s1, scalar1=1.0 / D, scalar2=None, op0=ALU.mult))
            b.op("vector", lambda v: v.tensor_scalar(
                out=s2, in0=s2, scalar1=1.0 / D, scalar2=1e-5, op0=ALU.mult,
                op1=ALU.add))
            b.drain("vector")
            b.op("vector", lambda v: v.tensor_mul(sc2, sc1, sc1))
            b.drain("vector")
            b.op("vector", lambda v: v.tensor_sub(s2, s2, sc2), inc=vr)
            b.wait("scalar", vr)
            b.op("scalar", lambda s: s.activation(out=sc2, in_=s2, func=AF.Ln))
            b.drain("scalar")
            b.op("scalar", lambda s: s.activation(
                out=s1, in_=sc2, func=AF.Exp, scale=-0.5), inc=ssem)
            b.wait("vector", ssem)
            b.op("vector", lambda v: v.tensor_copy(abrow[:, 0:512], s1))
            b.op("vector", lambda v: v.scalar_tensor_tensor(
                out=abrow[:, 512:1024], in0=sc1, scalar=-1.0, in1=s1,
                op0=ALU.mult, op1=ALU.mult), inc=ssem)
            b.drain("vector")
            b.wait("tensor", ssem)
            for i in range(2):
                bank = psb[i]
                if bev.count >= 2:
                    b.wait("tensor", bev, bev.count - 1)
                b.op("tensor", lambda t, i=i, bank=bank: t.matmul(
                    bank[:, :], hb["ones_row_bf"][0:1, :],
                    abrow[0:1, i * 512:(i + 1) * 512], start=True, stop=True),
                    inc=bmm)
                b.wait("vector", bmm)
                b.op("vector", lambda v, i=i, ch=ch, bank=bank: v.tensor_copy(
                    abb[:, i * NTOK + ch * 512: i * NTOK + ch * 512 + 512],
                    bank[:, :]), inc=bev)
        xnv = b.sem(pfx + "xnv")
        b.wait("vector", bev)
        for dt in range(DT):
            for ch in range(NCH):
                xv = xb[:, dt * NTOK + ch * 512: dt * NTOK + ch * 512 + 512]
                xo = xn[:, dt * NTOK + ch * 512: dt * NTOK + ch * 512 + 512]
                Av = abb[:, ch * 512:(ch + 1) * 512]
                Bv = abb[:, NTOK + ch * 512: NTOK + (ch + 1) * 512]
                b.op("vector", lambda v, xv=xv, Av=Av: v.tensor_mul(tmp[:, :], xv, Av))
                b.drain("vector")
                b.op("vector", lambda v, xo=xo, Bv=Bv: v.tensor_add(
                    xo, tmp[:, :], Bv), inc=xnv)
                b.drain("vector")
    b.sem_release(_sm + 1)
    return xnv



def build_l0_moe(b, xn, g_raw, wg_hbm, io, pfx="mo"):
    nc, hb = b.nc, b.hb
    _sm = b.sem_mark()
    NE, HID, MT = 8, 2048, 16
    NCH = NTOK // 512
    G = 8
    with ExitStack() as sc:
        E = sc.enter_context
        gate_bufs = ExitStack()
        Eg = gate_bufs.enter_context
        glog = Eg(nc.sbuf_tensor(pfx + "glog", [128, 32 * 8], F32))
        wtm = Eg(nc.sbuf_tensor(pfx + "wtm", [128, 32 * 8], BF16))
        w8 = Eg(nc.sbuf_tensor(pfx + "w8", [8, NTOK], BF16))
        vv = Eg(nc.sbuf_tensor(pfx + "vv", [128, G * 16], F32))
        cnt = Eg(nc.sbuf_tensor(pfx + "cnt", [128, G * 8], F32))
        t1b = Eg(nc.sbuf_tensor(pfx + "t1", [128, G * 8], F32))
        eex = Eg(nc.sbuf_tensor(pfx + "ee", [128, G * 8], F32))
        den = Eg(nc.sbuf_tensor(pfx + "den", [128, G], F32))
        gate_sb = Eg(nc.sbuf_tensor(pfx + "gate", [128, DT * 8], BF16))

        gate_scope = ExitStack()
        gps = gate_scope.enter_context(nc.psum_tensor(pfx + "gps", [128, 8], F32))
        tpsb = gate_scope.enter_context(nc.psum_tensor(pfx + "tpsb", [8, 128], BF16))
        gld = b.sem(pfx + "gld")
        b.dma("gpsimd", gate_sb[:, :].rearrange("p (kt n) -> p kt n", n=8),
              io["l0_gate"].rearrange("(kt p) n -> p kt n", p=128), gld)
        gmm = b.sem(pfx + "gmm")
        gev = b.sem(pfx + "gev")
        gev0 = gev.count
        b.wait("tensor", gld)
        for tt in range(32):
            if tt >= 1:
                b.wait("tensor", gev, gev0 + tt)
            for kt in range(DT):
                b.op("tensor", lambda t, tt=tt, kt=kt: t.matmul(
                    gps[:, :],
                    xn[:, kt * NTOK + tt * 128: kt * NTOK + tt * 128 + 128],
                    gate_sb[:, kt * 8:(kt + 1) * 8],
                    start=(kt == 0), stop=(kt == DT - 1)),
                    inc=(gmm if kt == DT - 1 else None))
            b.wait("vector", gmm)
            b.op("vector", lambda v, tt=tt: v.tensor_copy(
                glog[:, tt * 8:(tt + 1) * 8], gps[:, :]), inc=gev)
        wsem = b.sem(pfx + "ws")
        for gg in range(32 // G):
            gl = glog[:, gg * G * 8:(gg + 1) * G * 8].rearrange(
                "p (g n) -> p g n", n=8)
            glf = glog[:, gg * G * 8:(gg + 1) * G * 8]
            vvv = vv[:, :].rearrange("p (g n) -> p g n", n=16)
            b.op("vector", lambda v, glf=glf: v.tensor_copy(
                vv[:, :].rearrange("p (g n) -> p g n", n=16)[:, :, 0:8],
                glf.rearrange("p (g n) -> p g n", n=8)))
            b.op("vector", lambda v, glf=glf: v.tensor_copy(
                vv[:, :].rearrange("p (g n) -> p g n", n=16)[:, :, 8:16],
                glf.rearrange("p (g n) -> p g n", n=8)))
            b.drain("vector")
            cntv = cnt[:, :].rearrange("p (g n) -> p g n", n=8)
            t1v = t1b[:, :].rearrange("p (g n) -> p g n", n=8)
            b.op("vector", lambda v, gl=gl: v.tensor_tensor(
                cnt[:, :].rearrange("p (g n) -> p g n", n=8),
                vv[:, :].rearrange("p (g n) -> p g n", n=16)[:, :, 1:9],
                gl, ALU.is_gt))
            b.drain("vector")
            for s_ in range(2, 8):
                b.op("vector", lambda v, gl=gl, s_=s_: v.tensor_tensor(
                    t1b[:, :].rearrange("p (g n) -> p g n", n=8),
                    vv[:, :].rearrange("p (g n) -> p g n", n=16)[:, :, s_:s_ + 8],
                    gl, ALU.is_gt))
                b.drain("vector")
                b.op("vector", lambda v: v.tensor_add(cnt[:, :], cnt[:, :], t1b[:, :]))
                b.drain("vector")
            for s_ in range(1, 8):
                b.op("vector", lambda v, gl=gl, s_=s_: v.tensor_tensor(
                    t1b[:, :].rearrange("p (g n) -> p g n", n=8)[:, :, 8 - s_:8],
                    vv[:, :].rearrange("p (g n) -> p g n", n=16)[:, :, 8:8 + s_],
                    gl[:, :, 8 - s_:8], ALU.is_equal))
                b.drain("vector")
                b.op("vector", lambda v, s_=s_: v.tensor_add(
                    cnt[:, :].rearrange("p (g n) -> p g n", n=8)[:, :, 8 - s_:8],
                    cnt[:, :].rearrange("p (g n) -> p g n", n=8)[:, :, 8 - s_:8],
                    t1b[:, :].rearrange("p (g n) -> p g n", n=8)[:, :, 8 - s_:8]))
                b.drain("vector")
            vsm = b.sem(pfx + f"vsm{gg}")
            b.op("vector", lambda v: v.tensor_scalar(
                out=cnt[:, :], in0=cnt[:, :], scalar1=2.0, scalar2=None,
                op0=ALU.is_lt), inc=vsm)
            ssm = b.sem(pfx + f"ssm{gg}")
            b.wait("scalar", vsm)
            b.op("scalar", lambda s, glf=glf: s.activation(
                out=eex[:, :], in_=glf, func=AF.Exp), inc=ssm)
            b.wait("vector", ssm)
            b.op("vector", lambda v: v.tensor_mul(eex[:, :], eex[:, :], cnt[:, :]))
            b.drain("vector")
            b.op("vector", lambda v: v.tensor_reduce(
                den[:, :], eex[:, :].rearrange("p (g n) -> p g n", n=8),
                axis=AX.X, op=ALU.add))
            b.drain("vector")
            b.op("vector", lambda v: v.reciprocal(den[:, :], den[:, :]))
            b.drain("vector")
            b.op("vector", lambda v, gg=gg: v.tensor_tensor(
                wtm[:, gg * G * 8:(gg + 1) * G * 8].rearrange("p (g n) -> p g n", n=8),
                eex[:, :].rearrange("p (g n) -> p g n", n=8),
                den[:, :].rearrange("p (g o) -> p g o", o=1).broadcast_to([128, G, 8]),
                ALU.mult), inc=wsem)
            b.drain("vector")
        tmm = b.sem(pfx + "tmm")
        tev = b.sem(pfx + "tev")
        tev0 = tev.count
        b.wait("tensor", wsem)
        for tt in range(32):
            if tt >= 1:
                b.wait("tensor", tev, tev0 + tt)
            b.op("tensor", lambda t, tt=tt: t.transpose(
                tpsb[:, :], wtm[:, tt * 8:(tt + 1) * 8], hb["ident_bf"][:, :]),
                inc=tmm)
            b.wait("vector", tmm)
            b.op("vector", lambda v, tt=tt: v.tensor_copy(
                w8[:, tt * 128:(tt + 1) * 128], tpsb[0:8, 0:128]), inc=tev)
        fl = b.sem(pfx + "fl")
        b.wait("scalar", tev)
        b.dma("scalar", wg_hbm[:, :], w8[:, :], fl)
        gate_scope.close()
        gate_bufs.close()
        wrow = E(nc.sbuf_tensor(pfx + "wrow", [1, NTOK], BF16))
        xne = E(nc.sbuf_tensor(pfx + "xne", [128, DT * 512], BF16))
        wbc = E(nc.sbuf_tensor(pfx + "wbc", [128, 512], BF16))
        h0s = E(nc.sbuf_tensor(pfx + "h0s", [128, 512], F32))
        mbuf = E(nc.sbuf_tensor(pfx + "mbuf", [128, MT * 512], BF16))
        osb = E(nc.sbuf_tensor(pfx + "osb", [128, 2 * 512], F32))
        w0sb = E(nc.sbuf_tensor(pfx + "w0sb", [128, DT * 2048], BF16))
        w1sb = E(nc.sbuf_tensor(pfx + "w1sb", [128, DT * 2048], BF16))
        wosb = E(nc.sbuf_tensor(pfx + "wosb", [128, MT * 1024], BF16))
        wps = E(nc.psum_tensor(pfx + "wps", [128, 512], F32))
        h0p = E(nc.psum_tensor(pfx + "h0p", [128, 512], F32))
        h1p = E(nc.psum_tensor(pfx + "h1p", [128, 512], F32))
        acc = [E(nc.psum_tensor(pfx + f"acc{i}", [128, 512], F32)) for i in range(4)]

        wls = b.sem(pfx + "wls")
        wbev = b.sem(pfx + "wbev")
        xnv = b.sem(pfx + "xnv")
        h0d = b.sem(pfx + "h0d")
        h1d = b.sem(pfx + "h1d")
        sil = b.sem(pfx + "sil")
        mtt = b.sem(pfx + "mtt")
        omm = b.sem(pfx + "omm")
        oev = b.sem(pfx + "oev")
        ost = b.sem(pfx + "ost")
        wmm = b.sem(pfx + "wmm")

        def load_expert(e):
            for half in range(2):
                b.dma("gpsimd",
                      w0sb[:, half * 8192:(half + 1) * 8192].rearrange(
                          "p (kt m) -> p kt m", m=2048),
                      io["l0_w0"][e].rearrange("(kt p) m -> p kt m", p=128)
                      [:, half * 4:(half + 1) * 4, :], wls)
                b.dma("gpsimd",
                      w1sb[:, half * 8192:(half + 1) * 8192].rearrange(
                          "p (kt m) -> p kt m", m=2048),
                      io["l0_w1"][e].rearrange("(kt p) m -> p kt m", p=128)
                      [:, half * 4:(half + 1) * 4, :], wls)
                b.dma("gpsimd",
                      wosb[:, half * 8192:(half + 1) * 8192].rearrange(
                          "p (kt o) -> p kt o", o=1024),
                      io["l0_woe"][e].rearrange("(kt p) o -> p kt o", p=128)
                      [:, half * 8:(half + 1) * 8, :], wls)

        wrl = b.sem(pfx + "wrl")
        wrl0 = wrl.count
        wls0 = wls.count
        load_expert(0)
        b.wait("sync", fl)
        b.dma("sync", wrow[0:1, :], wg_hbm[0:1, :], wrl)
        for e in range(NE):
            if e > 0:
                b.wait("sync", wmm)
                b.dma("sync", wrow[0:1, :], wg_hbm[e:e + 1, :], wrl)
            b.wait("tensor", wrl, wrl0 + 16 * (e + 1))
            b.wait("tensor", wls, wls0 + 16 * 6 * (e + 1))
            for tc in range(NCH):
                if e + tc > 0:
                    b.wait("tensor", wbev)
                b.op("tensor", lambda t, e=e, tc=tc: t.matmul(
                    wps[:, :], hb["ones_row_bf"][0:1, :],
                    wrow[0:1, tc * 512: tc * 512 + 512],
                    start=True, stop=True), inc=wmm)
                b.wait("vector", wmm)
                if e + tc > 0:
                    b.wait("vector", h1d)
                b.op("vector", lambda v: v.tensor_copy(wbc[:, :], wps[:, :]),
                     inc=wbev)
                b.drain("vector")
                for dt in range(DT):
                    b.op("vector", lambda v, dt=dt, tc=tc: v.tensor_mul(
                        xne[:, dt * 512:(dt + 1) * 512],
                        xn[:, dt * NTOK + tc * 512: dt * NTOK + tc * 512 + 512],
                        wbc[:, :]), inc=(xnv if dt == DT - 1 else None))
                b.wait("tensor", xnv)
                for mt in range(MT):
                    if e + tc + mt > 0:
                        b.wait("tensor", sil)
                        b.wait("tensor", mtt)
                    for kt in range(DT):
                        b.op("tensor", lambda t, kt=kt, mt=mt, tc=tc: t.matmul(
                            h0p[:, :],
                            w0sb[:, kt * 2048 + mt * 128: kt * 2048 + mt * 128 + 128],
                            xn[:, kt * NTOK + tc * 512: kt * NTOK + tc * 512 + 512],
                            start=(kt == 0), stop=(kt == DT - 1)),
                            inc=(h0d if kt == DT - 1 else None))
                    for kt in range(DT):
                        b.op("tensor", lambda t, kt=kt, mt=mt: t.matmul(
                            h1p[:, :],
                            w1sb[:, kt * 2048 + mt * 128: kt * 2048 + mt * 128 + 128],
                            xne[:, kt * 512:(kt + 1) * 512],
                            start=(kt == 0), stop=(kt == DT - 1)),
                            inc=(h1d if kt == DT - 1 else None))
                    b.wait("scalar", h0d)
                    b.op("scalar", lambda s: s.activation(
                        out=h0s[:, :], in_=h0p[:, :], func=AF.Silu), inc=sil)
                    b.wait("vector", sil)
                    b.wait("vector", h1d)
                    b.op("vector", lambda v, mt=mt: v.tensor_mul(
                        mbuf[:, mt * 512:(mt + 1) * 512], h0s[:, :], h1p[:, :]),
                        inc=mtt)
                b.wait("tensor", mtt)
                for og in range(2):
                    b.wait("tensor", oev)
                    for odt in range(4):
                        o = og * 4 + odt
                        for kt in range(MT):
                            b.op("tensor", lambda t, kt=kt, o=o, odt=odt: t.matmul(
                                acc[odt][:, :],
                                wosb[:, kt * 1024 + o * 128: kt * 1024 + o * 128 + 128],
                                mbuf[:, kt * 512:(kt + 1) * 512],
                                start=(kt == 0), stop=(kt == MT - 1)),
                                inc=(omm if kt == MT - 1 else None))
                    for odt in range(4):
                        o = og * 4 + odt
                        b.wait("vector", omm, omm.count - 3 + odt)
                        if ost.count >= 32:
                            b.wait("vector", ost, ost.count - 16)
                        b.op("vector", lambda v, odt=odt: v.tensor_copy(
                            osb[:, (odt % 2) * 512:(odt % 2) * 512 + 512],
                            acc[odt][:, :]), inc=oev)
                        b.wait("gpsimd", oev)
                        dst = g_raw[o * 128:(o + 1) * 128, tc * 512:(tc + 1) * 512]
                        src = osb[:, (odt % 2) * 512:(odt % 2) * 512 + 512]
                        if e == 0:
                            b.dma("gpsimd", dst, src, ost)
                        else:
                            b.dma("gpsimd", dst, src, ost, accum_op=ALU.add)
            if e + 1 < NE:
                b.wait("gpsimd", mtt)
                b.wait("gpsimd", omm)
                load_expert(e + 1)
    b.sem_release(_sm + 1)
    return ost




def build_wo_combine(b, attno, wosb, g_raw, x_in, x_out, c0, pfx, sems, kdim=DT,
                     bias_pp=None):
    nc, hb = b.nc, b.hb
    prp, pj, sgd, cmb, cld, ostx, gch, xch2, xch = (
        sems["prp"], sems["pj"], sems["sgd"], sems["cmb"], sems["cld"],
        sems["ostx"], sems["gch"], sems["xch2"], sems["xch"])
    for ch in range(4):
        for odt in range(DT):
            b.wait("sync", cmb)
            b.wait("sync", sgd)
            b.dma("sync", gch[:, :],
                  g_raw[odt * 128:(odt + 1) * 128, c0 + ch * 512: c0 + ch * 512 + 512],
                  cld)
            b.dma("sync", xch2[:, :],
                  x_in[odt * 128:(odt + 1) * 128, c0 + ch * 512: c0 + ch * 512 + 512],
                  cld)
            b.wait("tensor", cmb)
            for kt in range(kdim):
                b.op("tensor", lambda t, kt=kt, odt=odt, ch=ch: t.matmul(
                    prp[:, :],
                    wosb[:, kt * D + odt * 128: kt * D + odt * 128 + 128],
                    attno[:, kt * T + ch * 512: kt * T + ch * 512 + 512],
                    start=(kt == 0), stop=(kt == kdim - 1)),
                    inc=(pj if kt == kdim - 1 else None))
            b.wait("scalar", cld)
            b.op("scalar", lambda s: s.activation(
                out=gch[:, :], in_=gch[:, :], func=AF.Silu), inc=sgd)
            b.wait("vector", pj)
            b.wait("vector", sgd)
            if bias_pp is None:
                b.op("vector", lambda v: v.tensor_mul(
                    xch[:, :], prp[:, :], gch[:, :]), inc=cmb)
            else:
                b.op("vector", lambda v, odt=odt: v.scalar_tensor_tensor(
                    out=xch[:, :], in0=prp[:, :],
                    scalar=bias_pp[:, odt:odt + 1], in1=gch[:, :],
                    op0=ALU.add, op1=ALU.mult), inc=cmb)
            b.drain("vector")
            b.op("vector", lambda v: v.tensor_add(
                gch[:, :], xch[:, :], xch2[:, :]), inc=cmb)
            b.drain("vector")
            b.wait("sync", cmb)
            b.dma("sync",
                  x_out[odt * 128:(odt + 1) * 128, c0 + ch * 512: c0 + ch * 512 + 512],
                  gch[:, :], ostx)



ATTN_CONSTS = ("u_bc", "uinv_bc", "wdec_col", "wdecinv_col", "dprime")


def build_l0_attn(b, xn, g_raw, x_in, x_out, io, pfx="at", upto="all"):
    nc, hb = b.nc, b.hb
    _sm = b.sem_mark()
    with ExitStack() as sc:
        E = sc.enter_context
        cl = b.sem(pfx + "cl")
        for name in ATTN_CONSTS:
            h = E(nc.sbuf_tensor(pfx + name, list(io[name].shape), CONST_DTYPES[name]))
            hb[name] = h
            b.dma("sync", h[:, :], io[name][:, :], cl)
        wqsb = E(nc.sbuf_tensor(pfx + "wq", [128, DT * D], BF16))
        wupks = E(nc.sbuf_tensor(pfx + "wupk", [128, 2 * D], BF16))
        wupvs = E(nc.sbuf_tensor(pfx + "wupv", [128, 2 * NPAIR * 130], BF16))
        vones = E(nc.sbuf_tensor(pfx + "vones", [1, NPAIR * 130], BF16))
        wosb = E(nc.sbuf_tensor(pfx + "wo", [128, DT * D], BF16))
        latsb = E(nc.sbuf_tensor(pfx + "lat", [128, 2 * NTOK], BF16))
        sigp = E(nc.sbuf_tensor(pfx + "sg", [1, 2 * T], BF16))
        rsp = sigp
        wld = b.sem(pfx + "wld")
        b.dma("gpsimd", wqsb[:, :].rearrange("p (kt m) -> p kt m", m=1024),
              io["l0_wq"].rearrange("(kt p) m -> p kt m", p=128), wld)
        b.dma("gpsimd", wupks[:, :].rearrange("p (kt m) -> p kt m", m=1024),
              io["l0_wupk"].rearrange("(kt p) m -> p kt m", p=128), wld)
        b.dma("gpsimd", wupvs[:, :].rearrange("p (kt m) -> p kt m", m=1040),
              io["l0_wupv_aug"].rearrange("(kt p) m -> p kt m", p=128), wld)
        b.dma("gpsimd", vones[:, :], io["l0_vaug_ones"][:, :], wld)
        b.dma("gpsimd", wosb[:, :].rearrange("p (kt m) -> p kt m", m=1024),
              io["l0_wo"].rearrange("(kt p) m -> p kt m", p=128), wld)

        pj = b.sem(pfx + "pj")
        pje = b.sem(pfx + "pje")

        with ExitStack() as sa:
            Ea = sa.enter_context
            wkvsb = Ea(nc.sbuf_tensor(pfx + "wkv", [128, DT * D], BF16))
            wdsb = Ea(nc.sbuf_tensor(pfx + "wd", [128, DT * 256], BF16))
            wtmp = Ea(nc.sbuf_tensor(pfx + "wtmp", [128, 2 * D], BF16))
            kvsb = Ea(nc.sbuf_tensor(pfx + "kvc", [128, DT * 512], BF16))
            prp = Ea(nc.psum_tensor(pfx + "prpA", [128, 512], F32))
            b.dma("gpsimd", wdsb[:, :].rearrange("p (kt m) -> p kt m", m=256),
                  io["l0_wdown"].rearrange("(kt p) m -> p kt m", p=128), wld)
            wkl = b.sem(pfx + "wkl")
            wkv_v = b.sem(pfx + "wkvv")
            wkv0 = wkv_v.count
            for kt in range(DT):
                if kt >= 1:
                    b.wait("gpsimd", wkv_v, wkv0 + kt)
                b.dma("gpsimd", wtmp[:, 0:D],
                      io["l0_wk"].rearrange("(kt p) m -> p kt m", p=128)[:, kt, :],
                      wkl)
                b.dma("gpsimd", wtmp[:, D:2 * D],
                      io["l0_wv"].rearrange("(kt p) m -> p kt m", p=128)[:, kt, :],
                      wkl)
                b.wait("vector", wkl)
                b.op("vector", lambda v, kt=kt: v.tensor_add(
                    wkvsb[:, kt * D:(kt + 1) * D], wtmp[:, 0:D], wtmp[:, D:2 * D]),
                    inc=wkv_v)
            b.wait("tensor", wkv_v)
            b.wait("tensor", wld)
            kve = b.sem(pfx + "kve")
            for bb in range(BLOC):
                c0 = bb * T
                for ch in range(4):
                    for mt in range(DT):
                        b.wait("tensor", pje)
                        for kt in range(DT):
                            b.op("tensor", lambda t, kt=kt, mt=mt, ch=ch, c0=c0:
                                 t.matmul(
                                     prp[:, :],
                                     wkvsb[:, kt * D + mt * 128: kt * D + mt * 128 + 128],
                                     xn[:, kt * NTOK + c0 + ch * 512: kt * NTOK + c0 + ch * 512 + 512],
                                     start=(kt == 0), stop=(kt == DT - 1)),
                                 inc=(pj if kt == DT - 1 else None))
                        b.wait("scalar", pj)
                        b.op("scalar", lambda s, mt=mt: s.copy(
                            kvsb[:, mt * 512:(mt + 1) * 512], prp[:, :]), inc=pje)
                    b.wait("tensor", pje)
                    for mt in range(2):
                        b.wait("tensor", pje)
                        for kt in range(DT):
                            b.op("tensor", lambda t, kt=kt, mt=mt: t.matmul(
                                prp[:, :],
                                wdsb[:, kt * 256 + mt * 128: kt * 256 + mt * 128 + 128],
                                kvsb[:, kt * 512:(kt + 1) * 512],
                                start=(kt == 0), stop=(kt == DT - 1)),
                                inc=(pj if kt == DT - 1 else None))
                        b.wait("scalar", pj)
                        b.op("scalar", lambda s, mt=mt, ch=ch, c0=c0: s.copy(
                            latsb[:, mt * NTOK + c0 + ch * 512: mt * NTOK + c0 + ch * 512 + 512],
                            prp[:, :]), inc=pje)
            b.barrier()

        if upto == "lat":
            fin0 = b.sem(pfx + "fin0")
            b.wait("gpsimd", pje)
            b.dma("gpsimd", x_out[0:128, :], latsb[:, 0 * NTOK:1 * NTOK], fin0)
            b.dma("gpsimd", x_out[128:256, :], latsb[:, 1 * NTOK:2 * NTOK], fin0)
            b.wait("gpsimd", fin0)
            return fin0
        attno = E(nc.sbuf_tensor(pfx + "ao", [128, DT * T], BF16))
        qp = E(nc.sbuf_tensor(pfx + "qp", [128, T], BF16))
        kp = E(nc.sbuf_tensor(pfx + "kp", [128, T], BF16))
        vw = E(nc.sbuf_tensor(pfx + "vw", [128, 16 * 130], BF16))
        vwi = E(nc.sbuf_tensor(pfx + "vwi", [128, 16 * 130], BF16))
        pt = E(nc.sbuf_tensor(pfx + "pt", [128, 2 * 512], BF16))
        t1b = pt[:, 0:512]
        t2b = pt[:, 512:1024]
        evt = E(nc.sbuf_tensor(pfx + "evt", [128, 512], F32))
        evt2 = E(nc.sbuf_tensor(pfx + "evt2", [128, 512], F32))
        gch = E(nc.sbuf_tensor(pfx + "gch", [128, 512], F32))
        rbc = pt[:, 0:512]
        xch = evt
        xch2 = evt2
        scp = [E(nc.psum_tensor(pfx + f"scp{i}", [128, 512], F32)) for i in range(2)]
        avp = [E(nc.psum_tensor(pfx + f"avp{i}", [65, 512], F32)) for i in range(4)]
        prp = E(nc.psum_tensor(pfx + "prp", [128, 512], F32))
        vps = E(nc.psum_tensor(pfx + "vps", [128, 130], F32))

        scm = b.sem(pfx + "scm")
        exd = b.sem(pfx + "exd")
        dtt = b.sem(pfx + "dtt")
        avm = b.sem(pfx + "avm")
        ave = b.sem(pfx + "ave")
        vsc = b.sem(pfx + "vsc")
        sgd = b.sem(pfx + "sgd")
        rmm = b.sem(pfx + "rmm")
        rev = b.sem(pfx + "rev")
        cmb = b.sem(pfx + "cmb")
        cld = b.sem(pfx + "cld")
        ostx = b.sem(pfx + "ostx")

        for bb in range(BLOC):
            c0 = bb * T
            for pr in range(NPAIR):
                b.wait("tensor", sgd)
                b.wait("tensor", cmb)
                for ch in range(4):
                    b.wait("tensor", pje)
                    for kt in range(DT):
                        b.op("tensor", lambda t, kt=kt, pr=pr, ch=ch, c0=c0: t.matmul(
                            prp[:, :],
                            wqsb[:, kt * D + pr * 128: kt * D + pr * 128 + 128],
                            xn[:, kt * NTOK + c0 + ch * 512: kt * NTOK + c0 + ch * 512 + 512],
                            start=(kt == 0), stop=(kt == DT - 1)),
                            inc=(pj if kt == DT - 1 else None))
                    b.wait("scalar", pj)
                    b.op("scalar", lambda s, ch=ch: s.copy(
                        qp[:, ch * 512:(ch + 1) * 512], prp[:, :]), inc=pje)
                for ch in range(4):
                    b.wait("tensor", pje)
                    for kt in range(2):
                        b.op("tensor", lambda t, kt=kt, pr=pr, ch=ch, c0=c0: t.matmul(
                            prp[:, :],
                            wupks[:, kt * D + pr * 128: kt * D + pr * 128 + 128],
                            latsb[:, kt * NTOK + c0 + ch * 512: kt * NTOK + c0 + ch * 512 + 512],
                            start=(kt == 0), stop=(kt == 1)),
                            inc=(pj if kt == 1 else None))
                    b.wait("scalar", pj)
                    b.op("scalar", lambda s, ch=ch: s.copy(
                        kp[:, ch * 512:(ch + 1) * 512], prp[:, :]), inc=pje)
                for tt2 in range(16):
                    if tt2 >= 1:
                        b.wait("tensor", vsc)
                    for kt in range(2):
                        b.op("tensor", lambda t, kt=kt, pr=pr, tt2=tt2, c0=c0:
                             t.matmul(
                                 vps[:, :],
                                 latsb[:, kt * NTOK + c0 + tt2 * 128: kt * NTOK + c0 + tt2 * 128 + 128],
                                 wupvs[:, kt * NPAIR * 130 + pr * 130: kt * NPAIR * 130 + (pr + 1) * 130],
                                 start=(kt == 0), stop=False))
                    b.op("tensor", lambda t, pr=pr: t.matmul(
                        vps[:, :], hb["ones_row_bf"][0:1, :],
                        vones[0:1, pr * 130:(pr + 1) * 130],
                        start=False, stop=True), inc=pj)
                    b.wait("vector", pj)
                    b.op("vector", lambda v, tt2=tt2: v.tensor_scalar(
                        out=vw[:, tt2 * 130:(tt2 + 1) * 130], in0=vps[:, :],
                        scalar1=hb["wdec_col"][:, tt2:tt2 + 1], scalar2=None,
                        op0=ALU.mult))
                    b.op("vector", lambda v, tt2=tt2: v.tensor_scalar(
                        out=vwi[:, tt2 * 130:(tt2 + 1) * 130], in0=vps[:, :],
                        scalar1=hb["wdecinv_col"][:, tt2:tt2 + 1], scalar2=None,
                        op0=ALU.mult), inc=vsc)
                    b.drain("vector")
                if upto == "qkv":
                    continue
                b.wait("tensor", vsc)
                b.wait("tensor", pje)
                for qc in range(4):
                    started = [False, False, False, False]

                    def avmm(idx, lhs, rh, colrange=None):
                        st = not started[idx]
                        started[idx] = True
                        if colrange is None:
                            b.op("tensor", lambda t, idx=idx, lhs=lhs, rh=rh, st=st:
                                 t.matmul(avp[idx][:, :], lhs, rh, start=st,
                                          stop=False, skip_group_check=True),
                                 inc=avm)
                        else:
                            a0, a1 = colrange
                            b.op("tensor", lambda t, idx=idx, lhs=lhs, rh=rh, st=st,
                                 a0=a0, a1=a1: t.matmul(
                                     avp[idx][:, a0:a1], lhs, rh, start=st,
                                     stop=False, skip_group_check=True),
                                 inc=avm)

                    for kt in range(16):
                        b.wait("tensor", exd)
                        for h2 in range(2):
                            b.op("tensor", lambda t, h2=h2, kt=kt, qc=qc: t.matmul(
                                scp[h2][:, :],
                                kp[h2 * 64:(h2 + 1) * 64, kt * 128:(kt + 1) * 128],
                                qp[h2 * 64:(h2 + 1) * 64, qc * 512:(qc + 1) * 512],
                                start=True, stop=True,
                                tile_position=(h2 * 64, 0)), inc=scm)
                        b.wait("scalar", scm)
                        b.wait("scalar", avm)
                        b.wait("scalar", dtt)
                        b.wait("scalar", rev)
                        for h2 in range(2):
                            b.op("scalar", lambda s, h2=h2: s.activation(
                                out=pt[:, h2 * 512:(h2 + 1) * 512],
                                in_=scp[h2][:, :], func=AF.Exp, scale=QK_SCALE),
                                inc=exd)
                        if upto == "score":
                            continue
                        if 4 * qc <= kt <= 4 * qc + 3:
                            sq = kt - 4 * qc
                            b.wait("vector", exd)
                            for h2 in range(2):
                                b.op("vector", lambda v, h2=h2, sq=sq: v.tensor_mul(
                                    pt[:, h2 * 512 + sq * 128: h2 * 512 + sq * 128 + 128],
                                    pt[:, h2 * 512 + sq * 128: h2 * 512 + sq * 128 + 128],
                                    hb["dprime"][:, :]), inc=dtt)
                            b.drain("vector")
                            b.wait("tensor", dtt)
                        else:
                            b.wait("tensor", exd)
                        for h2 in range(2):
                            vwsl = vw[:, kt * 130 + h2 * 65: kt * 130 + h2 * 65 + 65]
                            vwisl = vwi[:, kt * 130 + h2 * 65: kt * 130 + h2 * 65 + 65]
                            pth = pt[:, h2 * 512:(h2 + 1) * 512]
                            iL, iU = 2 * h2, 2 * h2 + 1
                            if kt < 4 * qc:
                                avmm(iL, vwsl, pth)
                            elif kt > 4 * qc + 3:
                                avmm(iU, vwisl, pth)
                            else:
                                for sq in range(4):
                                    qt = 4 * qc + sq
                                    sub = pth[:, sq * 128:(sq + 1) * 128]
                                    cr = (sq * 128, (sq + 1) * 128)
                                    if kt <= qt:
                                        avmm(iL, vwsl, sub, cr)
                                    else:
                                        avmm(iU, vwisl, sub, cr)
                    if upto in ("score", "avmm"):
                        continue
                    b.wait("vector", avm)
                    b.wait("vector", sgd)
                    for h2 in range(2):
                        iL, iU = 2 * h2, 2 * h2 + 1
                        hidx = 2 * pr + h2
                        b.op("vector", lambda v, iL=iL, qc=qc: v.tensor_mul(
                            evt[0:65, :], avp[iL][:, :],
                            hb["u_bc"][0:65, qc * 512:(qc + 1) * 512]))
                        b.op("vector", lambda v, iU=iU, qc=qc: v.tensor_mul(
                            evt2[0:65, :], avp[iU][:, :],
                            hb["uinv_bc"][0:65, qc * 512:(qc + 1) * 512]))
                        b.drain("vector")
                        dst = attno[h2 * 64:h2 * 64 + 64,
                                    pr * T + qc * 512: pr * T + qc * 512 + 512]
                        b.op("vector", lambda v, dst=dst: v.tensor_add(
                            dst, evt[0:64, :], evt2[0:64, :]))
                        b.op("vector", lambda v, h2=h2, qc=qc: v.tensor_add(
                            sigp[0:1, h2 * T + qc * 512: h2 * T + qc * 512 + 512],
                            evt[64:65, :], evt2[64:65, :]), inc=ave)
                        b.drain("vector")
                    b.wait("tensor", ave)
                if upto in ("qkv", "score", "avmm", "evac"):
                    continue
                b.wait("scalar", ave)
                b.op("scalar", lambda s: s.activation(
                    out=sigp[0:1, :], in_=sigp[0:1, :], func=AF.Ln))
                b.drain("scalar")
                b.op("scalar", lambda s: s.activation(
                    out=sigp[0:1, :], in_=sigp[0:1, :], func=AF.Exp, scale=-1.0),
                    inc=sgd)
                b.wait("tensor", sgd)
                for qc2 in range(4):
                    b.wait("tensor", rev)
                    for h2 in range(2):
                        b.op("tensor", lambda t, h2=h2, qc2=qc2: t.matmul(
                            prp[h2 * 64:(h2 + 1) * 64, :],
                            hb["ones_row_bf"][0:1, 0:64],
                            rsp[0:1, h2 * T + qc2 * 512: h2 * T + qc2 * 512 + 512],
                            start=True, stop=True,
                            tile_position=(0, h2 * 64)), inc=rmm)
                    b.wait("scalar", rmm)
                    b.wait("scalar", rev)
                    b.op("scalar", lambda s: s.copy(rbc, prp[:, :]), inc=sgd)
                    b.wait("vector", sgd)
                    dst = attno[:, pr * T + qc2 * 512: pr * T + qc2 * 512 + 512]
                    b.op("vector", lambda v, dst=dst: v.tensor_mul(
                        dst, dst, rbc), inc=rev)
                    b.drain("vector")
            if upto == "av":
                continue
            b.wait("tensor", rev)
            b.wait("tensor", sgd)
            build_wo_combine(b, attno, wosb, g_raw, x_in, x_out, c0, pfx,
                             dict(prp=prp, pj=pj, sgd=sgd, cmb=cmb, cld=cld,
                                  ostx=ostx, gch=gch, xch2=xch2, xch=xch))
        if upto in ("av", "qkv", "score", "avmm", "evac"):
            fin1 = b.sem(pfx + "fin1")
            b.wait("gpsimd", rev)
            for i_ in range(4):
                b.dma("gpsimd", x_out[i_ * 128:(i_ + 1) * 128, :],
                      attno[:, i_ * 4096:(i_ + 1) * 4096], fin1)
            b.wait("gpsimd", fin1)
            return fin1
        b.wait("sync", ostx)
    b.sem_release(_sm)
    return ostx


MAGIC = 12582912.0


def emit_quant(b, dst, src, qb, vsem):
    N = src.shape[-1]
    nb = N // 16
    at, q1, q2 = qb["at"][:, 0:N], qb["q1"][:, 0:N], qb["q2"][:, 0:N]
    tr = qb["tr"]
    bm = qb["bm"][:, 0:nb]

    def tt(out, i0, i1, op):
        b.op("vector", lambda v, out=out, i0=i0, i1=i1, op=op:
             v.tensor_tensor(out, i0, i1, op))
        b.drain("vector")

    def ts(out, i0, s1, op0, s2=None, op1=None):
        if s2 is None:
            b.op("vector", lambda v, out=out, i0=i0, s1=s1, op0=op0:
                 v.tensor_scalar(out=out, in0=i0, scalar1=s1, scalar2=None,
                                 op0=op0))
        else:
            b.op("vector", lambda v, out=out, i0=i0, s1=s1, op0=op0, s2=s2,
                 op1=op1: v.tensor_scalar(out=out, in0=i0, scalar1=s1,
                                          scalar2=s2, op0=op0, op1=op1))
        b.drain("vector")

    def stt(out, i0, sc, i1, op0, op1):
        b.op("vector", lambda v, out=out, i0=i0, sc=sc, i1=i1, op0=op0, op1=op1:
             v.scalar_tensor_tensor(out=out, in0=i0, scalar=sc, in1=i1,
                                    op0=op0, op1=op1))
        b.drain("vector")

    v3 = lambda ap, w: ap.rearrange("p (nb w) -> p nb w", w=w)
    stt(at, src, -1.0, src, ALU.mult, ALU.max)
    tt(v3(tr[:, 0:N // 2], 8), v3(at, 16)[:, :, 0:8], v3(at, 16)[:, :, 8:16], ALU.max)
    tt(v3(tr[:, 0:N // 4], 4), v3(tr[:, 0:N // 2], 8)[:, :, 0:4],
       v3(tr[:, 0:N // 2], 8)[:, :, 4:8], ALU.max)
    tt(v3(tr[:, 0:N // 8], 2), v3(tr[:, 0:N // 4], 4)[:, :, 0:2],
       v3(tr[:, 0:N // 4], 4)[:, :, 2:4], ALU.max)
    tt(bm, v3(tr[:, 0:N // 8], 2)[:, :, 0:1].rearrange("p nb o -> p (nb o)"),
       v3(tr[:, 0:N // 8], 2)[:, :, 1:2].rearrange("p nb o -> p (nb o)"), ALU.max)
    ts(bm, bm, 1.0 / 6.0, ALU.mult, 1e-12, ALU.max)
    b.op("vector", lambda v: v.reciprocal(qb["rs"][:, 0:nb], bm))
    b.drain("vector")
    rsb = qb["rs"][:, 0:nb].rearrange("p (nb o) -> p nb o", o=1).broadcast_to(
        [128, nb, 16])
    bmb = bm.rearrange("p (nb o) -> p nb o", o=1).broadcast_to([128, nb, 16])
    tt(v3(at, 16), v3(at, 16), rsb, ALU.mult)
    ts(q1, at, -2.0, ALU.add, 0.0, ALU.max)
    stt(q1, q1, -0.5, at, ALU.mult, ALU.add)
    ts(q2, at, -4.0, ALU.add, 0.0, ALU.max)
    stt(q1, q2, -0.25, q1, ALU.mult, ALU.add)
    ts(q1, q1, 2.0, ALU.mult, MAGIC, ALU.add)
    ts(q2, q1, -(MAGIC + 4.0), ALU.add, 0.0, ALU.max)
    ts(at, q1, -(MAGIC + 6.0), ALU.add, 0.0, ALU.max)
    ts(q1, q1, -MAGIC, ALU.add, 0.5, ALU.mult)
    stt(q1, q2, 0.5, q1, ALU.mult, ALU.add)
    tt(q1, q1, at, ALU.add)
    tt(v3(q1, 16), v3(q1, 16), bmb, ALU.mult)
    ts(at, src, 0.0, ALU.is_ge, 2.0, ALU.mult)
    tt(q2, q1, at, ALU.mult)
    b.op("vector", lambda v, dst=dst: v.tensor_tensor(dst, q2, q1, ALU.subtract),
         inc=vsem)
    b.drain("vector")



def build_l1_ffn(b, xn, g_raw, scr, io, pfx="f1", upto="all"):
    nc, hb = b.nc, b.hb
    _sm = b.sem_mark()
    I = 2816
    IT = 22
    NCHN = [(i * 512, 512) for i in range(5)] + [(2560, 256)]
    with ExitStack() as sc:
        E = sc.enter_context
        qb = dict(at=E(nc.sbuf_tensor(pfx + "qat", [128, 2816], F32)),
                  q1=E(nc.sbuf_tensor(pfx + "qq1", [128, 2816], F32)),
                  q2=E(nc.sbuf_tensor(pfx + "qq2", [128, 2816], F32)),
                  tr=E(nc.sbuf_tensor(pfx + "qtr", [128, 1408], F32)),
                  bm=E(nc.sbuf_tensor(pfx + "qbm", [128, 176], F32)),
                  rs=E(nc.sbuf_tensor(pfx + "qrs", [128, 176], F32)))
        stage = E(nc.sbuf_tensor(pfx + "stage", [128, 2816], F32))
        qbf = E(nc.sbuf_tensor(pfx + "qbf", [128, 2816], BF16))
        tstg = E(nc.sbuf_tensor(pfx + "tstg", [128, 2816], BF16))
        tps = E(nc.psum_tensor(pfx + "tps", [128, 128], BF16))
        ld = b.sem(pfx + "ld")
        vq = b.sem(pfx + "vq")
        tm = b.sem(pfx + "tm")
        te = b.sem(pfx + "te")
        st = b.sem(pfx + "st")

        for wname, wq_h, nrt, ncol in (("l1_wg", scr["wgq"], IT, D),
                                       ("l1_wu", scr["wuq"], IT, D),
                                       ("l1_wd", scr["wdq"], DT, I)):
            for rt in range(nrt):
                b.wait("sync", vq)
                b.dma("sync", stage[:, 0:ncol], io[wname][rt * 128:(rt + 1) * 128, :],
                      ld)
                b.wait("vector", ld)
                b.wait("vector", tm)
                emit_quant(b, qbf[:, 0:ncol], stage[:, 0:ncol], qb, vq)
                nct = ncol // 128
                b.wait("tensor", vq)
                for kt in range(nct):
                    b.wait("tensor", te)
                    b.op("tensor", lambda t, kt=kt: t.transpose(
                        tps[:, :], qbf[:, kt * 128:(kt + 1) * 128],
                        hb["ident_bf"][:, :]), inc=tm)
                    b.wait("vector", tm)
                    b.op("vector", lambda v, kt=kt: v.tensor_copy(
                        tstg[:, kt * 128:(kt + 1) * 128], tps[:, :]), inc=te)
                b.wait("gpsimd", te)
                b.dma("gpsimd",
                      wq_h[:, rt * 128:(rt + 1) * 128].rearrange(
                          "(kt p) m -> p kt m", p=128),
                      tstg[:, 0:ncol].rearrange("p (kt m) -> p kt m", m=128), st)
        for tt2 in range(32 if upto != "A" else 0):
            b.wait("tensor", vq)
            for kt in range(DT):
                b.wait("tensor", te)
                b.op("tensor", lambda t, kt=kt, tt2=tt2: t.transpose(
                    tps[:, :], xn[:, kt * NTOK + tt2 * 128: kt * NTOK + tt2 * 128 + 128],
                    hb["ident_bf"][:, :]), inc=tm)
                b.wait("vector", tm)
                b.op("vector", lambda v, kt=kt: v.tensor_copy(
                    stage[:, kt * 128:(kt + 1) * 128], tps[:, :]), inc=te)
            b.wait("vector", te)
            b.wait("vector", st, st.count)
            emit_quant(b, qbf[:, 0:D], stage[:, 0:D], qb, vq)
            b.wait("gpsimd", vq)
            b.dma("gpsimd", scr["xq"][tt2 * 128:(tt2 + 1) * 128, :], qbf[:, 0:D], st)
        b.barrier()
    if upto in ("AB", "A"):
        b.sem_release(_sm)
        return None

    with ExitStack() as sc:
        E = sc.enter_context
        qb = dict(at=E(nc.sbuf_tensor(pfx + "cat", [128, 512], F32)),
                  q1=E(nc.sbuf_tensor(pfx + "cq1", [128, 512], F32)),
                  q2=E(nc.sbuf_tensor(pfx + "cq2", [128, 512], F32)),
                  tr=E(nc.sbuf_tensor(pfx + "ctr", [128, 256], F32)),
                  bm=E(nc.sbuf_tensor(pfx + "cbm", [128, 32], F32)),
                  rs=E(nc.sbuf_tensor(pfx + "crs", [128, 32], F32)))
        xqf = E(nc.sbuf_tensor(pfx + "xqf", [128, DT * NTOK], BF16))
        wgc = E(nc.sbuf_tensor(pfx + "wgc", [128, DT * 512], BF16))
        wuc = E(nc.sbuf_tensor(pfx + "wuc", [128, DT * 512], BF16))
        gsb = E(nc.sbuf_tensor(pfx + "gsb", [128, 512], BF16))
        hf = E(nc.sbuf_tensor(pfx + "hf", [128, 512], F32))
        hqb = E(nc.sbuf_tensor(pfx + "hqb", [128, 512], BF16))
        gps = E(nc.psum_tensor(pfx + "gps", [128, 512], F32))
        ups = E(nc.psum_tensor(pfx + "ups", [128, 512], F32))
        ld = b.sem(pfx + "cld")
        mmg = b.sem(pfx + "cmmg")
        mmu = b.sem(pfx + "cmmu")
        sil = b.sem(pfx + "csil")
        htt = b.sem(pfx + "chtt")
        vq = b.sem(pfx + "cvq")
        st2 = b.sem(pfx + "cst")
        for kt in range(DT):
            b.dma("sync", xqf[:, kt * NTOK:(kt + 1) * NTOK],
                  scr["xq"][:, kt * 128:(kt + 1) * 128], ld, transpose=True)
        b.wait("tensor", ld)
        for nc0, nw in NCHN:
            b.wait("sync", mmg)
            b.wait("sync", mmu)
            for kt in range(DT):
                b.dma("sync", wgc[:, kt * 512:kt * 512 + nw],
                      scr["wgq"][kt * 128:(kt + 1) * 128, nc0:nc0 + nw], ld)
                b.dma("sync", wuc[:, kt * 512:kt * 512 + nw],
                      scr["wuq"][kt * 128:(kt + 1) * 128, nc0:nc0 + nw], ld)
            b.wait("tensor", ld)
            for tt2 in range(32):
                if tt2 > 0 or nc0 > 0:
                    b.wait("tensor", sil)
                    b.wait("tensor", htt)
                for kt in range(DT):
                    b.op("tensor", lambda t, kt=kt, tt2=tt2, nw=nw: t.matmul(
                        gps[:, 0:nw], xqf[:, kt * NTOK + tt2 * 128: kt * NTOK + tt2 * 128 + 128],
                        wgc[:, kt * 512:kt * 512 + nw],
                        start=(kt == 0), stop=(kt == DT - 1)),
                        inc=(mmg if kt == DT - 1 else None))
                for kt in range(DT):
                    b.op("tensor", lambda t, kt=kt, tt2=tt2, nw=nw: t.matmul(
                        ups[:, 0:nw], xqf[:, kt * NTOK + tt2 * 128: kt * NTOK + tt2 * 128 + 128],
                        wuc[:, kt * 512:kt * 512 + nw],
                        start=(kt == 0), stop=(kt == DT - 1)),
                        inc=(mmu if kt == DT - 1 else None))
                b.wait("scalar", mmg)
                b.wait("scalar", htt)
                b.op("scalar", lambda s, nw=nw: s.activation(
                    out=gsb[:, 0:nw], in_=gps[:, 0:nw], func=AF.Silu), inc=sil)
                b.wait("vector", sil)
                b.wait("vector", mmu)
                b.op("vector", lambda v, nw=nw: v.tensor_mul(
                    hf[:, 0:nw], gsb[:, 0:nw], ups[:, 0:nw]), inc=htt)
                b.drain("vector")
                b.wait("vector", st2, st2.count)
                emit_quant(b, hqb[:, 0:nw], hf[:, 0:nw], qb, vq)
                b.wait("gpsimd", vq)
                b.dma("gpsimd", scr["hq"][tt2 * 128:(tt2 + 1) * 128, nc0:nc0 + nw],
                      hqb[:, 0:nw], st2)
        b.barrier()
    if upto == "C":
        b.sem_release(_sm)
        return None

    with ExitStack() as sc:
        E = sc.enter_context
        wdsb = E(nc.sbuf_tensor(pfx + "wdsb", [128, IT * D], BF16))
        hqc = E(nc.sbuf_tensor(pfx + "hqc", [128, IT * 512], BF16))
        osb = E(nc.sbuf_tensor(pfx + "osb", [128, 2 * 512], F32))
        dps = [E(nc.psum_tensor(pfx + f"dps{i}", [128, 512], F32)) for i in range(2)]
        ld = b.sem(pfx + "dld")
        hld = b.sem(pfx + "dhld")
        mm = b.sem(pfx + "dmm")
        ev = b.sem(pfx + "dev")
        st3 = b.sem(pfx + "dst")
        b.dma("sync", wdsb[:, :].rearrange("p (kt m) -> p kt m", m=D),
              scr["wdq"].rearrange("(kt p) m -> p kt m", p=128), ld)
        b.wait("tensor", ld)
        for tc in range(8):
            b.wait("sync", mm)
            for kt in range(IT):
                b.dma("sync", hqc[:, kt * 512:(kt + 1) * 512],
                      scr["hq"][tc * 512:(tc + 1) * 512, kt * 128:(kt + 1) * 128],
                      hld, transpose=True)
            b.wait("tensor", hld)
            for mt in range(DT):
                sl = mt % 2
                if ev.count >= 1:
                    b.wait("tensor", ev, ev.count - 1)
                for kt in range(IT):
                    b.op("tensor", lambda t, kt=kt, mt=mt, sl=sl: t.matmul(
                        dps[sl][:, :],
                        wdsb[:, kt * D + mt * 128: kt * D + mt * 128 + 128],
                        hqc[:, kt * 512:(kt + 1) * 512],
                        start=(kt == 0), stop=(kt == IT - 1)),
                        inc=(mm if kt == IT - 1 else None))
                b.wait("vector", mm)
                if st3.count >= 32:
                    b.wait("vector", st3, st3.count - 16)
                b.op("vector", lambda v, sl=sl: v.tensor_copy(
                    osb[:, sl * 512:(sl + 1) * 512], dps[sl][:, :]), inc=ev)
                b.wait("gpsimd", ev)
                b.dma("gpsimd", g_raw[mt * 128:(mt + 1) * 128, tc * 512:(tc + 1) * 512],
                      osb[:, sl * 512:(sl + 1) * 512], st3)
        b.barrier()
    b.sem_release(_sm)
    return st3



L1_CONSTS = ("cos_tab", "sin_tab", "r2t")


def build_l1_attn(b, xn, g_raw, x_in, x_out, io, pfx="a1", upto="all"):
    LV = {"proj": 1, "norm": 2, "hat": 3, "v": 4, "pass1": 5,
          "mrow": 6, "pass2": 7, "sig": 8, "all": 9}[upto]
    nc, hb = b.nc, b.hb
    _sm = b.sem_mark()
    with ExitStack() as sc:
        E = sc.enter_context
        cl = b.sem(pfx + "cl")
        for name in L1_CONSTS:
            h = E(nc.sbuf_tensor(pfx + name, list(io[name].shape), CONST_DTYPES[name]))
            hb[name] = h
            b.dma("sync", h[:, :], io[name][:, :], cl)
        woT = E(nc.sbuf_tensor(pfx + "woT", [128, DT * D], BF16))
        sqk = E(nc.sbuf_tensor(pfx + "sqk", [128, 8], F32))
        wqs = E(nc.sbuf_tensor(pfx + "wqs", [128, DT * 128], BF16))
        wks = E(nc.sbuf_tensor(pfx + "wks", [128, DT * 128], BF16))
        wvs = E(nc.sbuf_tensor(pfx + "wvs", [128, DT * 130], BF16))
        vone = E(nc.sbuf_tensor(pfx + "vone", [1, 130], BF16))
        qrb = E(nc.sbuf_tensor(pfx + "qrb", [128, T], BF16))
        krb = E(nc.sbuf_tensor(pfx + "krb", [128, T], BF16))
        khat = E(nc.sbuf_tensor(pfx + "khat", [128, T], BF16))

        qaug = [E(nc.sbuf_tensor(pfx + f"qaug{i}", [65, T], BF16)) for i in range(2)]
        kaug = [E(nc.sbuf_tensor(pfx + f"kaug{i}", [65, T], BF16)) for i in range(2)]
        vaug = E(nc.sbuf_tensor(pfx + "vaug", [128, 16 * 130], BF16))
        pt = E(nc.sbuf_tensor(pfx + "pt", [128, 2 * 512], BF16))
        t1b = pt[:, 0:512]
        t2b = pt[:, 512:1024]
        w2col = E(nc.sbuf_tensor(pfx + "w2col", [128, 4], F32))
        wq2 = E(nc.sbuf_tensor(pfx + "wq2", [128, 2], BF16))
        qnrow = E(nc.sbuf_tensor(pfx + "qnrow", [1, 512], F32))
        crow = [E(nc.sbuf_tensor(pfx + f"crow{i}", [1, T], BF16)) for i in range(2)]
        knrow = [E(nc.sbuf_tensor(pfx + f"knrow{i}", [1, T], BF16)) for i in range(2)]
        mcol = E(nc.sbuf_tensor(pfx + "mcol", [128, 2 * 16], BF16))
        mtmp = E(nc.sbuf_tensor(pfx + "mtmp", [128, 1], F32))
        msta = E(nc.sbuf_tensor(pfx + "msta", [16, 128], BF16))
        mrow = E(nc.sbuf_tensor(pfx + "mrow", [1, T], BF16))
        cbb = E(nc.sbuf_tensor(pfx + "cbb", [128, 512], BF16))
        attno = E(nc.sbuf_tensor(pfx + "ao", [128, DT * T], BF16))
        sigp = E(nc.sbuf_tensor(pfx + "sg", [1, 2 * T], BF16))
        evt = E(nc.sbuf_tensor(pfx + "evt", [128, 512], F32))
        evt2 = E(nc.sbuf_tensor(pfx + "evt2", [128, 512], F32))
        gch = evt
        xch = evt2
        xch2 = E(nc.sbuf_tensor(pfx + "xch2", [128, 512], F32))
        rbc = pt[:, 0:512]
        scp = [E(nc.psum_tensor(pfx + f"scp{i}", [128, 512], F32)) for i in range(2)]
        avp = [E(nc.psum_tensor(pfx + f"avp{i}", [65, 512], F32)) for i in range(2)]
        prp = E(nc.psum_tensor(pfx + "prp", [128, 512], F32))
        vps = E(nc.psum_tensor(pfx + "vps", [128, 130], F32))
        nps = E(nc.psum_tensor(pfx + "nps", [1, 512], F32))
        tpp = E(nc.psum_tensor(pfx + "tpp", [128, 128], BF16))

        wld = b.sem(pfx + "wld")
        b.dma("gpsimd", woT[:, :].rearrange("p (kt m) -> p kt m", m=1024),
              io["l1_woT"].rearrange("(kt p) m -> p kt m", p=128), wld)
        b.dma("sync", sqk[:, :], io["l1_sqk"][:, :], wld)

        pj = b.sem(pfx + "pj")
        pje = b.sem(pfx + "pje")
        rr = b.sem(pfx + "rr")
        nrm = b.sem(pfx + "nrm")
        trd = b.sem(pfx + "trd")
        mxd = b.sem(pfx + "mxd")
        dmv = b.sem(pfx + "dmv")
        scm = b.sem(pfx + "scm")
        exd = b.sem(pfx + "exd")
        avm = b.sem(pfx + "avm")
        ave = b.sem(pfx + "ave")
        sgd = b.sem(pfx + "sgd")
        rmm = b.sem(pfx + "rmm")
        rev = b.sem(pfx + "rev")
        cmb = b.sem(pfx + "cmb")
        cld = b.sem(pfx + "cld")
        ostx = b.sem(pfx + "ostx")
        wsl = b.sem(pfx + "wsl")
        vsc = b.sem(pfx + "vsc")

        b.wait("tensor", wld)
        b.wait("tensor", cl)
        b.wait("vector", cl)
        b.wait("vector", wld)
        ms = b.sem(pfx + "ms")
        for i in range(2):
            b.op("vector", lambda v, i=i: v.memset(qaug[i][64:65, :], 1.0))
            b.op("vector", lambda v, i=i: v.memset(kaug[i][64:65, :], 1.0))
        b.op("vector", lambda v: v.memset(mcol[:, :], 0.0), inc=ms)
        b.drain("vector")

        for bb2 in range(BLOC):
            c0 = bb2 * T
            for pr in range(NPAIR):
                b.wait("gpsimd", pj)
                b.wait("vector", sgd)
                b.wait("vector", rev)
                b.wait("scalar", rev)
                b.wait("tensor", sgd)
                b.wait("tensor", cmb)
                b.dma("gpsimd", wqs[:, :].rearrange("p (kt m) -> p kt m", m=128),
                      io["l1_wqT"].rearrange("(kt p) m -> p kt m", p=128)
                      [:, :, pr * 128:(pr + 1) * 128], wsl)
                b.dma("gpsimd", wks[:, :].rearrange("p (kt m) -> p kt m", m=128),
                      io["l1_wkT"].rearrange("(kt p) m -> p kt m", p=128)
                      [:, :, pr * 128:(pr + 1) * 128], wsl)
                b.dma("gpsimd", wvs[:, :].rearrange("p (kt m) -> p kt m", m=130),
                      io["l1_wvT_aug"].rearrange("(kt p) m -> p kt m", p=128)
                      [:, :, pr * 130:(pr + 1) * 130], wsl)
                b.dma("gpsimd", vone[0:1, :],
                      io["l0_vaug_ones"][0:1, pr * 130:(pr + 1) * 130], wsl)
                b.op("vector", lambda v, pr=pr: v.tensor_scalar(
                    out=w2col[:, 0:1], in0=sqk[:, pr:pr + 1], scalar1=32.0,
                    scalar2=None, op0=ALU.mult))
                b.drain("vector")
                b.op("vector", lambda v: v.tensor_mul(
                    w2col[:, 1:2], w2col[:, 0:1], w2col[:, 0:1]))
                b.drain("vector")
                b.op("vector", lambda v: v.reciprocal(w2col[:, 2:3], w2col[:, 1:2]))
                b.drain("vector")
                b.op("vector", lambda v: v.memset(wq2[:, :], 0.0))
                b.drain("vector")
                b.op("vector", lambda v: v.tensor_copy(
                    wq2[0:64, 0:1], w2col[0:64, 2:3]))
                b.drain("vector")
                b.op("vector", lambda v: v.tensor_copy(
                    wq2[64:128, 1:2], w2col[64:128, 2:3]), inc=nrm)
                b.drain("vector")
                b.wait("tensor", wsl)
                for wsb, dst in ((wqs, qrb), (wks, krb)):
                    for ch in range(4):
                        b.wait("tensor", pje)
                        b.wait("tensor", rr)
                        for kt in range(DT):
                            b.op("tensor", lambda t, kt=kt, ch=ch, wsb=wsb, c0=c0:
                                 t.matmul(
                                     prp[:, :],
                                     wsb[:, kt * 128:(kt + 1) * 128],
                                     xn[:, kt * NTOK + c0 + ch * 512: kt * NTOK + c0 + ch * 512 + 512],
                                     start=(kt == 0), stop=(kt == DT - 1)),
                                 inc=(pj if kt == DT - 1 else None))
                        b.wait("scalar", pj)
                        b.op("scalar", lambda s, ch=ch, dst=dst: s.copy(
                            dst[:, ch * 512:(ch + 1) * 512], prp[:, :]), inc=pje)
                        b.wait("tensor", pje)
                        b.op("tensor", lambda t, ch=ch, dst=dst: t.matmul(
                            prp[:, :], hb["r2t"][:, :],
                            dst[:, ch * 512:(ch + 1) * 512],
                            start=True, stop=True), inc=rmm)
                        b.wait("vector", rmm)
                        b.op("vector", lambda v, ch=ch, dst=dst: v.tensor_mul(
                            t1b, dst[:, ch * 512:(ch + 1) * 512],
                            hb["cos_tab"][:, ch * 512:(ch + 1) * 512]))
                        b.op("vector", lambda v, ch=ch: v.tensor_mul(
                            t2b, prp[:, :],
                            hb["sin_tab"][:, ch * 512:(ch + 1) * 512]), inc=rr)
                        b.drain("vector")
                        b.op("vector", lambda v: v.tensor_add(
                            t1b, t1b, t2b))
                        b.drain("vector")
                        b.op("vector", lambda v, ch=ch, dst=dst: v.tensor_scalar(
                            out=dst[:, ch * 512:(ch + 1) * 512], in0=t1b,
                            scalar1=w2col[:, 0:1], scalar2=None, op0=ALU.mult),
                            inc=rr)
                        b.drain("vector")
                if LV < 2:
                    continue
                b.wait("vector", rr)
                for side, src, outrows, mul8 in ((0, qrb, crow, True),
                                                 (1, krb, knrow, False)):
                    b.wait("vector", nrm)
                    b.op("vector", lambda v, src=src: v.tensor_mul(
                        khat[:, 0:T], src[:, :], src[:, :]), inc=nrm)
                    b.drain("vector")
                    b.wait("tensor", nrm)
                    for h2 in range(2):
                        dstrow = outrows[h2]
                        for ch in range(4):
                            b.wait("tensor", nrm)
                            b.op("tensor", lambda t, h2=h2, ch=ch: t.matmul(
                                nps[:, :], wq2[:, h2:h2 + 1],
                                khat[:, ch * 512:(ch + 1) * 512],
                                start=True, stop=True), inc=scm)
                            b.wait("vector", scm)
                            b.op("vector", lambda v: v.tensor_scalar(
                                out=qnrow[0:1, :], in0=nps[0:1, :], scalar1=1e-24,
                                scalar2=None, op0=ALU.add), inc=nrm)
                            b.drain("vector")
                            b.wait("scalar", nrm)
                            b.op("scalar", lambda s: s.activation(
                                out=qnrow[0:1, :], in_=qnrow[0:1, :], func=AF.Ln))
                            b.drain("scalar")
                            b.op("scalar", lambda s, dstrow=dstrow, ch=ch:
                                 s.activation(
                                     out=dstrow[0:1, ch * 512:(ch + 1) * 512],
                                     in_=qnrow[0:1, :], func=AF.Exp,
                                     scale=-0.5), inc=nrm)
                            b.wait("vector", nrm)
                            if mul8:
                                b.op("vector", lambda v, dstrow=dstrow, ch=ch:
                                     v.tensor_scalar(
                                         out=dstrow[0:1, ch * 512:(ch + 1) * 512],
                                         in0=dstrow[0:1, ch * 512:(ch + 1) * 512],
                                         scalar1=8.0, scalar2=None, op0=ALU.mult),
                                     inc=nrm)
                                b.drain("vector")
                if LV < 3:
                    continue
                b.wait("tensor", nrm)
                for ch in range(4):
                    b.wait("tensor", rev)
                    for h2 in range(2):
                        b.op("tensor", lambda t, h2=h2, ch=ch: t.matmul(
                            prp[h2 * 64:(h2 + 1) * 64, :],
                            hb["ones_row_bf"][0:1, 0:64],
                            knrow[h2][0:1, ch * 512:(ch + 1) * 512],
                            start=True, stop=True, tile_position=(0, h2 * 64)),
                            inc=rmm)
                    b.wait("scalar", rmm)
                    b.op("scalar", lambda s: s.copy(cbb[:, :], prp[:, :]), inc=sgd)
                    b.wait("vector", sgd)
                    b.op("vector", lambda v, ch=ch: v.tensor_mul(
                        khat[:, ch * 512:(ch + 1) * 512],
                        krb[:, ch * 512:(ch + 1) * 512], cbb[:, :]), inc=rev)
                    b.drain("vector")
                    b.wait("tensor", rev)
                    for h2 in range(2):
                        b.op("tensor", lambda t, h2=h2, ch=ch: t.matmul(
                            prp[h2 * 64:(h2 + 1) * 64, :],
                            hb["ones_row_bf"][0:1, 0:64],
                            crow[h2][0:1, ch * 512:(ch + 1) * 512],
                            start=True, stop=True, tile_position=(0, h2 * 64)),
                            inc=rmm)
                    b.wait("scalar", rmm)
                    b.op("scalar", lambda s: s.copy(cbb[:, :], prp[:, :]), inc=sgd)
                    b.wait("vector", sgd)
                    for h2 in range(2):
                        b.op("vector", lambda v, ch=ch, h2=h2: v.tensor_mul(
                            qaug[h2][0:64, ch * 512:(ch + 1) * 512],
                            qrb[h2 * 64:(h2 + 1) * 64, ch * 512:(ch + 1) * 512],
                            cbb[h2 * 64:(h2 + 1) * 64, :]),
                            inc=(rev if h2 == 1 else None))
                    b.drain("vector")
                for h2 in range(2):
                    b.op("vector", lambda v, h2=h2: v.tensor_copy(
                        kaug[h2][0:64, :], khat[h2 * 64:(h2 + 1) * 64, :]), inc=rev)
                b.drain("vector")
                if LV < 4:
                    continue
                for tt2 in range(16):
                    if tt2 >= 1:
                        b.wait("tensor", vsc)
                    for kt in range(DT):
                        b.op("tensor", lambda t, kt=kt, tt2=tt2, c0=c0: t.matmul(
                            vps[:, :],
                            xn[:, kt * NTOK + c0 + tt2 * 128: kt * NTOK + c0 + tt2 * 128 + 128],
                            wvs[:, kt * 130:(kt + 1) * 130],
                            start=(kt == 0), stop=False, skip_group_check=True))
                    b.op("tensor", lambda t: t.matmul(
                        vps[:, :], hb["ones_row_bf"][0:1, :],
                        vone[0:1, 0:130],
                        start=False, stop=True, skip_group_check=True), inc=pj)
                    b.wait("vector", pj)
                    b.op("vector", lambda v, tt2=tt2: v.tensor_copy(
                        vaug[:, tt2 * 130:(tt2 + 1) * 130], vps[:, :]), inc=vsc)
                    b.drain("vector")
                if LV < 5:
                    continue
                b.wait("vector", ms)
                b.wait("tensor", rev)
                b.wait("tensor", vsc)
                for qt in range(16):
                    for ch in range(4):
                        b.wait("tensor", mxd)
                        for h2 in range(2):
                            b.op("tensor", lambda t, h2=h2, qt=qt, ch=ch: t.matmul(
                                scp[h2][:, :],
                                qrb[h2 * 64:(h2 + 1) * 64, qt * 128:(qt + 1) * 128],
                                khat[h2 * 64:(h2 + 1) * 64, ch * 512:(ch + 1) * 512],
                                start=True, stop=True, tile_position=(h2 * 64, 0)),
                                inc=scm)
                        b.wait("vector", scm)
                        for h2 in range(2):
                            b.op("vector", lambda v, h2=h2: v.reduce_max(
                                mtmp[:, :], scp[h2][:, :], axis=AX.X))
                            b.drain("vector")
                            col = mcol[:, h2 * 16 + qt:h2 * 16 + qt + 1]
                            if ch == 0:
                                b.op("vector", lambda v, col=col: v.tensor_copy(
                                    col, mtmp[:, :]), inc=mxd)
                            else:
                                b.op("vector", lambda v, col=col: v.tensor_tensor(
                                    col, col, mtmp[:, :], ALU.max), inc=mxd)
                            b.drain("vector")
                if LV < 6:
                    continue
                for h2 in range(2):
                    b.wait("tensor", mxd)
                    b.wait("tensor", dmv)
                    b.op("tensor", lambda t, h2=h2: t.transpose(
                        tpp[0:16, :], mcol[:, h2 * 16:(h2 + 1) * 16],
                        hb["ident_bf"][:, :]), inc=trd)
                    b.wait("vector", trd)
                    b.op("vector", lambda v: v.tensor_copy(
                        msta[:, :], tpp[0:16, 0:128]), inc=trd)
                    b.drain("vector")
                    b.wait("sync", trd)
                    b.dma("sync", mrow[0:1, :], msta[:, :], dmv)
                    b.wait("vector", dmv)
                    b.op("vector", lambda v, h2=h2: v.scalar_tensor_tensor(
                        out=qaug[h2][64:65, :], in0=mrow[0:1, :], scalar=-1.0,
                        in1=crow[h2][0:1, :], op0=ALU.mult, op1=ALU.mult), inc=dmv)
                    b.drain("vector")
                b.op("vector", lambda v: v.memset(mcol[:, :], 0.0), inc=ms)
                b.drain("vector")
                if LV < 7:
                    continue
                b.wait("tensor", dmv)
                for qc in range(4):
                    for kt in range(16):
                        b.wait("tensor", exd)
                        for h2 in range(2):
                            b.op("tensor", lambda t, h2=h2, kt=kt, qc=qc: t.matmul(
                                scp[h2][:, :],
                                kaug[h2][:, kt * 128:(kt + 1) * 128],
                                qaug[h2][:, qc * 512:(qc + 1) * 512],
                                start=True, stop=True), inc=scm)
                        b.wait("scalar", scm)
                        b.wait("scalar", avm)
                        b.wait("scalar", rev)
                        for h2 in range(2):
                            b.op("scalar", lambda s, h2=h2: s.activation(
                                out=pt[:, h2 * 512:(h2 + 1) * 512],
                                in_=scp[h2][:, :], func=AF.Exp), inc=exd)
                        b.wait("tensor", exd)
                        for h2 in range(2):
                            b.op("tensor", lambda t, h2=h2, kt=kt: t.matmul(
                                avp[h2][:, :],
                                vaug[:, kt * 130 + h2 * 65: kt * 130 + h2 * 65 + 65],
                                pt[:, h2 * 512:(h2 + 1) * 512],
                                start=(kt == 0), stop=(kt == 15)), inc=avm)
                    b.wait("vector", avm)
                    b.wait("vector", sgd)
                    for h2 in range(2):
                        dst = attno[h2 * 64:h2 * 64 + 64,
                                    pr * T + qc * 512: pr * T + qc * 512 + 512]
                        b.op("vector", lambda v, dst=dst, h2=h2: v.tensor_copy(
                            dst, avp[h2][0:64, :]))
                        b.op("vector", lambda v, h2=h2, qc=qc: v.tensor_copy(
                            sigp[0:1, h2 * T + qc * 512: h2 * T + qc * 512 + 512],
                            avp[h2][64:65, :]), inc=ave)
                        b.drain("vector")
                    b.wait("tensor", ave)
                if LV < 8:
                    continue
                b.wait("scalar", ave)
                b.op("scalar", lambda s: s.activation(
                    out=sigp[0:1, :], in_=sigp[0:1, :], func=AF.Ln))
                b.drain("scalar")
                b.op("scalar", lambda s: s.activation(
                    out=sigp[0:1, :], in_=sigp[0:1, :], func=AF.Exp, scale=-1.0),
                    inc=sgd)
                b.wait("tensor", sgd)
                for qc2 in range(4):
                    b.wait("tensor", rev)
                    for h2 in range(2):
                        b.op("tensor", lambda t, h2=h2, qc2=qc2: t.matmul(
                            prp[h2 * 64:(h2 + 1) * 64, :],
                            hb["ones_row_bf"][0:1, 0:64],
                            sigp[0:1, h2 * T + qc2 * 512: h2 * T + qc2 * 512 + 512],
                            start=True, stop=True, tile_position=(0, h2 * 64)),
                            inc=rmm)
                    b.wait("scalar", rmm)
                    b.wait("scalar", rev)
                    b.op("scalar", lambda s: s.copy(rbc, prp[:, :]), inc=sgd)
                    b.wait("vector", sgd)
                    dst = attno[:, pr * T + qc2 * 512: pr * T + qc2 * 512 + 512]
                    b.op("vector", lambda v, dst=dst: v.tensor_mul(
                        dst, dst, rbc), inc=rev)
                    b.drain("vector")
            if LV < 9:
                continue
            b.wait("tensor", rev)
            b.wait("tensor", sgd)
            build_wo_combine(b, attno, woT, g_raw, x_in, x_out, c0, pfx,
                             dict(prp=prp, pj=pj, sgd=sgd, cmb=cmb, cld=cld,
                                  ostx=ostx, gch=gch, xch2=xch2, xch=xch))
        if LV >= 9:
            b.wait("sync", ostx)
    b.sem_release(_sm)
    return ostx



def build_l2_ffn(b, xn, g_raw, io, pfx="f2"):
    nc, hb = b.nc, b.hb
    _sm = b.sem_mark()
    with ExitStack() as sc:
        E = sc.enter_context
        wr = E(nc.sbuf_tensor(pfx + "wr", [128, DT * D], BF16))
        wk = E(nc.sbuf_tensor(pfx + "wk", [128, DT * D], BF16))
        wv = E(nc.sbuf_tensor(pfx + "wv", [128, DT * D], BF16))
        mur = E(nc.sbuf_tensor(pfx + "mur", [128, 8], F32))
        muk = E(nc.sbuf_tensor(pfx + "muk", [128, 8], F32))
        mur1 = E(nc.sbuf_tensor(pfx + "mur1", [128, 16], F32))
        xrk = E(nc.sbuf_tensor(pfx + "xrk", [128, 2 * DT * 512], BF16))
        rch = E(nc.sbuf_tensor(pfx + "rch", [128, DT * 512], BF16))
        k2c = E(nc.sbuf_tensor(pfx + "k2c", [128, DT * 512], BF16))
        kre = E(nc.sbuf_tensor(pfx + "kre", [128, 512], BF16))
        osb = E(nc.sbuf_tensor(pfx + "osb", [128, 2 * 512], F32))
        prp = E(nc.psum_tensor(pfx + "prp", [128, 512], F32))
        ld = b.sem(pfx + "ld")
        vx = b.sem(pfx + "vx")
        pj = b.sem(pfx + "pj")
        pje = b.sem(pfx + "pje")
        ktt = b.sem(pfx + "ktt")
        oev = b.sem(pfx + "oev")
        ost = b.sem(pfx + "ost")
        for nm, buf in (("l2_wrT", wr), ("l2_wkT", wk), ("l2_wvT", wv)):
            b.dma("gpsimd", buf[:, :].rearrange("p (kt m) -> p kt m", m=D),
                  io[nm].rearrange("(kt p) m -> p kt m", p=128), ld)
        b.dma("sync", mur[:, :], io["l2_mur"][:, :], ld)
        b.dma("sync", muk[:, :], io["l2_muk"][:, :], ld)
        b.wait("vector", ld)
        b.op("vector", lambda v: v.tensor_scalar(
            out=mur1[:, 0:8], in0=mur[:, :], scalar1=-1.0, scalar2=1.0,
            op0=ALU.mult, op1=ALU.add))
        b.op("vector", lambda v: v.tensor_scalar(
            out=mur1[:, 8:16], in0=muk[:, :], scalar1=-1.0, scalar2=1.0,
            op0=ALU.mult, op1=ALU.add), inc=vx)
        b.drain("vector")
        b.wait("tensor", ld)
        for bb2 in range(BLOC):
            c0 = bb2 * T
            for ch in range(4):
                b.wait("vector", pj)
                for dt in range(DT):
                    xv = xn[:, dt * NTOK + c0 + ch * 512: dt * NTOK + c0 + ch * 512 + 512]
                    if ch == 0:
                        xvp = xn[:, dt * NTOK + c0: dt * NTOK + c0 + 511]
                        for off, mu_ in ((0, mur), (DT * 512, muk)):
                            dst = xrk[:, off + dt * 512: off + dt * 512 + 512]
                            b.op("vector", lambda v, dst=dst, xv=xv, mu_=mu_, dt=dt:
                                 v.tensor_scalar(out=dst, in0=xv,
                                                 scalar1=mu_[:, dt:dt + 1],
                                                 scalar2=None, op0=ALU.mult))
                            b.drain("vector")
                        for off, mucol in ((0, 0), (DT * 512, 8)):
                            dst = xrk[:, off + dt * 512 + 1: off + dt * 512 + 512]
                            b.op("vector", lambda v, dst=dst, xvp=xvp, mucol=mucol,
                                 dt=dt: v.scalar_tensor_tensor(
                                     out=dst, in0=xvp,
                                     scalar=mur1[:, mucol + dt:mucol + dt + 1],
                                     in1=dst, op0=ALU.mult, op1=ALU.add))
                            b.drain("vector")
                    else:
                        xvp = xn[:, dt * NTOK + c0 + ch * 512 - 1: dt * NTOK + c0 + ch * 512 + 511]
                        for off, mudt, mucol in ((0, mur, 0), (DT * 512, muk, 8)):
                            dst = xrk[:, off + dt * 512: off + dt * 512 + 512]
                            b.op("vector", lambda v, dst=dst, xv=xv, mudt=mudt, dt=dt:
                                 v.tensor_scalar(out=dst, in0=xv,
                                                 scalar1=mudt[:, dt:dt + 1],
                                                 scalar2=None, op0=ALU.mult))
                            b.drain("vector")
                            b.op("vector", lambda v, dst=dst, xvp=xvp, mucol=mucol,
                                 dt=dt: v.scalar_tensor_tensor(
                                     out=dst, in0=xvp,
                                     scalar=mur1[:, mucol + dt:mucol + dt + 1],
                                     in1=dst, op0=ALU.mult, op1=ALU.add))
                            b.drain("vector")
                b.op("vector", lambda v: v.engine_nop(), inc=vx)
                b.wait("tensor", vx)
                for mt in range(DT):
                    b.wait("tensor", pje)
                    for kt in range(DT):
                        b.op("tensor", lambda t, kt=kt, mt=mt: t.matmul(
                            prp[:, :], wr[:, kt * D + mt * 128: kt * D + mt * 128 + 128],
                            xrk[:, kt * 512:(kt + 1) * 512],
                            start=(kt == 0), stop=(kt == DT - 1)),
                            inc=(pj if kt == DT - 1 else None))
                    b.wait("scalar", pj)
                    b.op("scalar", lambda s, mt=mt: s.activation(
                        out=rch[:, mt * 512:(mt + 1) * 512], in_=prp[:, :],
                        func=AF.Sigmoid), inc=pje)
                for mt in range(DT):
                    b.wait("tensor", pje)
                    b.wait("tensor", ktt)
                    for kt in range(DT):
                        b.op("tensor", lambda t, kt=kt, mt=mt: t.matmul(
                            prp[:, :], wk[:, kt * D + mt * 128: kt * D + mt * 128 + 128],
                            xrk[:, DT * 512 + kt * 512: DT * 512 + (kt + 1) * 512],
                            start=(kt == 0), stop=(kt == DT - 1)),
                            inc=(pj if kt == DT - 1 else None))
                    b.wait("scalar", pj)
                    b.op("scalar", lambda s: s.activation(
                        out=kre[:, :], in_=prp[:, :], func=AF.Relu), inc=pje)
                    b.wait("vector", pje)
                    b.op("vector", lambda v, mt=mt: v.tensor_mul(
                        k2c[:, mt * 512:(mt + 1) * 512], kre[:, :], kre[:, :]),
                        inc=ktt)
                    b.drain("vector")
                b.wait("tensor", ktt)
                for mt in range(DT):
                    sl = mt % 2
                    b.wait("tensor", oev)
                    for kt in range(DT):
                        b.op("tensor", lambda t, kt=kt, mt=mt: t.matmul(
                            prp[:, :], wv[:, kt * D + mt * 128: kt * D + mt * 128 + 128],
                            k2c[:, kt * 512:(kt + 1) * 512],
                            start=(kt == 0), stop=(kt == DT - 1)),
                            inc=(pj if kt == DT - 1 else None))
                    b.wait("vector", pj)
                    if ost.count >= 32:
                        b.wait("vector", ost, ost.count - 16)
                    b.op("vector", lambda v, mt=mt, sl=sl: v.tensor_mul(
                        osb[:, sl * 512:(sl + 1) * 512], prp[:, :],
                        rch[:, mt * 512:(mt + 1) * 512]), inc=oev)
                    b.drain("vector")
                    b.wait("gpsimd", oev)
                    b.dma("gpsimd",
                          g_raw[mt * 128:(mt + 1) * 128, c0 + ch * 512: c0 + ch * 512 + 512],
                          osb[:, sl * 512:(sl + 1) * 512], ost)
        b.wait("gpsimd", ost)
        b.barrier()
    b.sem_release(_sm)


def build_l2_attn(b, xn, g_raw, x_in, x_out, io, pfx="a2"):
    nc, hb = b.nc, b.hb
    _sm = b.sem_mark()
    with ExitStack() as sc:
        E = sc.enter_context
        wqkv = E(nc.sbuf_tensor(pfx + "wqkv", [128, DT * 256], BF16))
        bqkv = E(nc.sbuf_tensor(pfx + "bqkv", [128, 16], F32))
        wout = E(nc.sbuf_tensor(pfx + "wout", [128, DT * D], BF16))
        bout = E(nc.sbuf_tensor(pfx + "bout", [128, 8], F32))
        wvs = E(nc.sbuf_tensor(pfx + "wvs", [128, DT * 130], BF16))
        vone = E(nc.sbuf_tensor(pfx + "vone", [1, 130], BF16))
        qp = E(nc.sbuf_tensor(pfx + "qp", [128, T], BF16))
        kp = E(nc.sbuf_tensor(pfx + "kp", [128, T], BF16))
        vaug = E(nc.sbuf_tensor(pfx + "vaug", [128, 16 * 130], BF16))
        pt = E(nc.sbuf_tensor(pfx + "pt", [128, 2 * 512], BF16))
        attno = E(nc.sbuf_tensor(pfx + "ao", [128, DT * T], BF16))
        sigp = E(nc.sbuf_tensor(pfx + "sg", [1, 2 * T], BF16))
        evt = E(nc.sbuf_tensor(pfx + "evt", [128, 512], F32))
        evt2 = E(nc.sbuf_tensor(pfx + "evt2", [128, 512], F32))
        xch2 = E(nc.sbuf_tensor(pfx + "xch2", [128, 512], F32))
        gch = evt
        xch = evt2
        rbc = pt[:, 0:512]
        scp = [E(nc.psum_tensor(pfx + f"scp{i}", [128, 512], F32)) for i in range(2)]
        avp = [E(nc.psum_tensor(pfx + f"avp{i}", [65, 512], F32)) for i in range(2)]
        prp = E(nc.psum_tensor(pfx + "prp", [128, 512], F32))
        vps = E(nc.psum_tensor(pfx + "vps", [128, 130], F32))

        wld = b.sem(pfx + "wld")
        b.dma("gpsimd", wout[:, :].rearrange("p (kt m) -> p kt m", m=D),
              io["l2_wout"].rearrange("(kt p) m -> p kt m", p=128), wld)
        b.dma("sync", bqkv[:, :], io["l2_bqkv"][:, :], wld)
        b.dma("sync", bout[:, :], io["l2_bout"][:, :], wld)

        pj = b.sem(pfx + "pj")
        pje = b.sem(pfx + "pje")
        vsc = b.sem(pfx + "vsc")
        scm = b.sem(pfx + "scm")
        exd = b.sem(pfx + "exd")
        dtt = b.sem(pfx + "dtt")
        avm = b.sem(pfx + "avm")
        ave = b.sem(pfx + "ave")
        sgd = b.sem(pfx + "sgd")
        rmm = b.sem(pfx + "rmm")
        rev = b.sem(pfx + "rev")
        cmb = b.sem(pfx + "cmb")
        cld = b.sem(pfx + "cld")
        ostx = b.sem(pfx + "ostx")
        wsl = b.sem(pfx + "wsl")

        b.wait("tensor", wld)
        for bb2 in range(BLOC):
            c0 = bb2 * T
            for pr in range(NPAIR):
                glob = pr < 2
                b.wait("gpsimd", pj)
                b.wait("vector", sgd)
                b.wait("vector", rev)
                b.wait("scalar", rev)
                b.wait("tensor", sgd)
                b.wait("tensor", cmb)
                wq3 = wqkv[:, :].rearrange("p (kt m) -> p kt m", m=256)
                b.dma("gpsimd", wq3[:, :, 0:128],
                      io["l2_wqkv"].rearrange("(kt p) m -> p kt m", p=128)
                      [:, :, pr * 128:pr * 128 + 128], wsl)
                b.dma("gpsimd", wq3[:, :, 128:256],
                      io["l2_wqkv"].rearrange("(kt p) m -> p kt m", p=128)
                      [:, :, 1024 + pr * 128:1024 + pr * 128 + 128], wsl)
                b.dma("gpsimd", wvs[:, :].rearrange("p (kt m) -> p kt m", m=130),
                      io["l2_wv_aug"].rearrange("(kt p) m -> p kt m", p=128)
                      [:, :, pr * 130:(pr + 1) * 130], wsl)
                b.dma("gpsimd", vone[0:1, :],
                      io["l2_vrow"][0:1, pr * 130:(pr + 1) * 130], wsl)
                b.wait("tensor", wsl)
                for which, coloff, bcol in ((0, 0, 0), (1, 1024, 8)):
                    dstb = qp if which == 0 else kp
                    for ch in range(4):
                        b.wait("tensor", pje)
                        for kt in range(DT):
                            b.op("tensor", lambda t, kt=kt, ch=ch, which=which, c0=c0:
                                 t.matmul(
                                     prp[:, :],
                                     wqkv[:, kt * 256 + which * 128: kt * 256 + which * 128 + 128],
                                     xn[:, kt * NTOK + c0 + ch * 512: kt * NTOK + c0 + ch * 512 + 512],
                                     start=(kt == 0), stop=(kt == DT - 1)),
                                 inc=(pj if kt == DT - 1 else None))
                        b.wait("scalar", pj)
                        b.op("scalar", lambda s, ch=ch, dstb=dstb, bcol=bcol, pr=pr:
                             s.activation(
                                 out=dstb[:, ch * 512:(ch + 1) * 512], in_=prp[:, :],
                                 func=AF.Identity,
                                 bias=bqkv[:, bcol + pr:bcol + pr + 1]), inc=pje)
                for tt2 in range(16):
                    if tt2 >= 1:
                        b.wait("tensor", vsc)
                    for kt in range(DT):
                        b.op("tensor", lambda t, kt=kt, tt2=tt2, c0=c0: t.matmul(
                            vps[:, :],
                            xn[:, kt * NTOK + c0 + tt2 * 128: kt * NTOK + c0 + tt2 * 128 + 128],
                            wvs[:, kt * 130:(kt + 1) * 130],
                            start=(kt == 0), stop=False, skip_group_check=True))
                    b.op("tensor", lambda t: t.matmul(
                        vps[:, :], hb["ones_row_bf"][0:1, :], vone[0:1, 0:130],
                        start=False, stop=True, skip_group_check=True), inc=pj)
                    b.wait("vector", pj)
                    b.op("vector", lambda v, tt2=tt2: v.tensor_copy(
                        vaug[:, tt2 * 130:(tt2 + 1) * 130], vps[:, :]), inc=vsc)
                    b.drain("vector")
                b.wait("tensor", vsc)
                b.wait("tensor", pje)
                for qc in range(4):
                    started = [False, False]

                    def avmm2(idx, lhs, rh, colrange=None):
                        st = not started[idx]
                        started[idx] = True
                        if colrange is None:
                            b.op("tensor", lambda t, idx=idx, lhs=lhs, rh=rh, st=st:
                                 t.matmul(avp[idx][:, :], lhs, rh, start=st,
                                          stop=False, skip_group_check=True),
                                 inc=avm)
                        else:
                            a0, a1 = colrange
                            b.op("tensor", lambda t, idx=idx, lhs=lhs, rh=rh, st=st,
                                 a0=a0, a1=a1: t.matmul(
                                     avp[idx][:, a0:a1], lhs, rh, start=st,
                                     stop=False, skip_group_check=True),
                                 inc=avm)

                    kts = (range(0, 4 * qc + 4) if glob
                           else range(max(0, 4 * qc - 2), 4 * qc + 4))
                    for kt in kts:
                        b.wait("tensor", exd)
                        for h2 in range(2):
                            b.op("tensor", lambda t, h2=h2, kt=kt, qc=qc: t.matmul(
                                scp[h2][:, :],
                                kp[h2 * 64:(h2 + 1) * 64, kt * 128:(kt + 1) * 128],
                                qp[h2 * 64:(h2 + 1) * 64, qc * 512:(qc + 1) * 512],
                                start=True, stop=True,
                                tile_position=(h2 * 64, 0)), inc=scm)
                        b.wait("scalar", scm)
                        b.wait("scalar", avm)
                        b.wait("scalar", rev)
                        b.wait("scalar", dtt)
                        for h2 in range(2):
                            b.op("scalar", lambda s, h2=h2: s.activation(
                                out=pt[:, h2 * 512:(h2 + 1) * 512],
                                in_=scp[h2][:, :], func=AF.Exp, scale=0.125),
                                inc=exd)
                        need_mask = []
                        for sq in range(4):
                            qt = 4 * qc + sq
                            d = qt - kt
                            if d == 0:
                                need_mask.append((sq, "mdiag"))
                            elif (not glob) and d == 2:
                                need_mask.append((sq, "mband"))
                        if need_mask:
                            b.wait("vector", exd)
                            for sq, mk in need_mask:
                                for h2 in range(2):
                                    b.op("vector", lambda v, h2=h2, sq=sq, mk=mk:
                                         v.tensor_mul(
                                             pt[:, h2 * 512 + sq * 128: h2 * 512 + sq * 128 + 128],
                                             pt[:, h2 * 512 + sq * 128: h2 * 512 + sq * 128 + 128],
                                             hb[mk][:, :]), inc=dtt)
                            b.drain("vector")
                            b.wait("tensor", dtt)
                        else:
                            b.wait("tensor", exd)
                        for h2 in range(2):
                            vsl = vaug[:, kt * 130 + h2 * 65: kt * 130 + h2 * 65 + 65]
                            pth = pt[:, h2 * 512:(h2 + 1) * 512]
                            if glob:
                                if kt < 4 * qc:
                                    avmm2(h2, vsl, pth)
                                else:
                                    for sq in range(4):
                                        if kt <= 4 * qc + sq:
                                            avmm2(h2, vsl,
                                                  pth[:, sq * 128:(sq + 1) * 128],
                                                  (sq * 128, (sq + 1) * 128))
                            else:
                                for sq in range(4):
                                    qt = 4 * qc + sq
                                    if 0 <= qt - kt <= 2:
                                        avmm2(h2, vsl,
                                              pth[:, sq * 128:(sq + 1) * 128],
                                              (sq * 128, (sq + 1) * 128))
                    b.wait("vector", avm)
                    b.wait("vector", sgd)
                    for h2 in range(2):
                        dst = attno[h2 * 64:h2 * 64 + 64,
                                    pr * T + qc * 512: pr * T + qc * 512 + 512]
                        b.op("vector", lambda v, dst=dst, h2=h2: v.tensor_copy(
                            dst, avp[h2][0:64, :]))
                        b.op("vector", lambda v, h2=h2, qc=qc: v.tensor_copy(
                            sigp[0:1, h2 * T + qc * 512: h2 * T + qc * 512 + 512],
                            avp[h2][64:65, :]), inc=ave)
                        b.drain("vector")
                    b.wait("tensor", ave)
                b.wait("scalar", ave)
                b.op("scalar", lambda s: s.activation(
                    out=sigp[0:1, :], in_=sigp[0:1, :], func=AF.Ln))
                b.drain("scalar")
                b.op("scalar", lambda s: s.activation(
                    out=sigp[0:1, :], in_=sigp[0:1, :], func=AF.Exp, scale=-1.0),
                    inc=sgd)
                b.wait("tensor", sgd)
                for qc2 in range(4):
                    b.wait("tensor", rev)
                    for h2 in range(2):
                        b.op("tensor", lambda t, h2=h2, qc2=qc2: t.matmul(
                            prp[h2 * 64:(h2 + 1) * 64, :],
                            hb["ones_row_bf"][0:1, 0:64],
                            sigp[0:1, h2 * T + qc2 * 512: h2 * T + qc2 * 512 + 512],
                            start=True, stop=True, tile_position=(0, h2 * 64)),
                            inc=rmm)
                    b.wait("scalar", rmm)
                    b.wait("scalar", rev)
                    b.op("scalar", lambda s: s.copy(rbc, prp[:, :]), inc=sgd)
                    b.wait("vector", sgd)
                    dst = attno[:, pr * T + qc2 * 512: pr * T + qc2 * 512 + 512]
                    b.op("vector", lambda v, dst=dst: v.tensor_mul(
                        dst, dst, rbc), inc=rev)
                    b.drain("vector")
            b.wait("tensor", rev)
            b.wait("tensor", sgd)
            build_wo_combine(b, attno, wout, g_raw, x_in, x_out, c0, pfx,
                             dict(prp=prp, pj=pj, sgd=sgd, cmb=cmb, cld=cld,
                                  ostx=ostx, gch=gch, xch2=xch2, xch=xch),
                             bias_pp=bout)
        b.wait("sync", ostx)
    b.sem_release(_sm)
    return ostx


def build_l2(b, xn, g_raw, x_in, x_out, io):
    s_ln = build_ln(b, xn, x_in, "ln2")
    for e in ENGINES:
        b.wait(e, s_ln)
    b.barrier()
    build_l2_ffn(b, xn, g_raw, io)
    b.barrier()
    build_l2_attn(b, xn, g_raw, x_in, x_out, io)
    b.barrier()



L3_CONSTS = ("tri64", "tri128s", "ones512", "ident_f")


def build_l3_ffn(b, xn, g_raw, io, pfx="f3"):
    nc, hb = b.nc, b.hb
    _sm = b.sem_mark()
    LOGK = math.log(1024 * 9.0 / 8.0)
    with ExitStack() as sc:
        E = sc.enter_context
        cl = b.sem(pfx + "cl")
        lb1 = E(nc.sbuf_tensor(pfx + "lb1", [128, 4], F32))
        hb1 = E(nc.sbuf_tensor(pfx + "hb1", [128, 32], F32))
        lb2r = E(nc.sbuf_tensor(pfx + "lb2r", [1, D], BF16))
        hb2r = E(nc.sbuf_tensor(pfx + "hb2r", [1, D], BF16))
        lg = E(nc.sbuf_tensor(pfx + "lg", [128, 8], F32))
        hg = E(nc.sbuf_tensor(pfx + "hg", [128, 8], F32))
        rtc = E(nc.sbuf_tensor(pfx + "rtc", [128, DT], BF16))
        mrow = [E(nc.sbuf_tensor(pfx + f"mrow{i}", [1, T], BF16)) for i in range(2)]
        nrow = E(nc.sbuf_tensor(pfx + "nrow", [1, 512], F32))
        rnrow = E(nc.sbuf_tensor(pfx + "rnrow", [1, 512], BF16))
        rnb = E(nc.sbuf_tensor(pfx + "rnb", [128, 512], BF16))
        maskb = E(nc.sbuf_tensor(pfx + "maskb", [128, 512], BF16))
        x2c = E(nc.sbuf_tensor(pfx + "x2c", [128, 512], BF16))
        nps = E(nc.psum_tensor(pfx + "nps", [1, 512], F32))
        bps = E(nc.psum_tensor(pfx + "bps", [128, 512], F32))
        rscope = ExitStack()
        Er = rscope.enter_context
        srow = [Er(nc.sbuf_tensor(pfx + f"srow{i}", [1, T], F32)) for i in range(2)]
        sbrow = [Er(nc.sbuf_tensor(pfx + f"sbrow{i}", [1, T], F32)) for i in range(2)]
        escr = Er(nc.sbuf_tensor(pfx + "escr", [1, T], BF16))
        tot = Er(nc.sbuf_tensor(pfx + "tot", [1, 4], F32))
        scb = Er(nc.sbuf_tensor(pfx + "scb", [128, T], F32))
        junk = Er(nc.sbuf_tensor(pfx + "junk", [128, T], BF16))
        stg16 = Er(nc.sbuf_tensor(pfx + "stg16", [16, 128], F32))
        sctm = Er(nc.sbuf_tensor(pfx + "sctm", [128, 16], F32))
        cntc = Er(nc.sbuf_tensor(pfx + "cntc", [128, 16], F32))
        satb = Er(nc.sbuf_tensor(pfx + "satb", [128, 16], BF16))
        fine = Er(nc.sbuf_tensor(pfx + "finee", [128, 16], F32))
        cbe = Er(nc.sbuf_tensor(pfx + "cbe", [128, 16], F32))
        rowa = Er(nc.sbuf_tensor(pfx + "rowa", [1, 64], F32))
        msk16 = Er(nc.sbuf_tensor(pfx + "msk16", [16, 128], BF16))
        t16 = Er(nc.psum_tensor(pfx + "t16", [128, 16], F32))
        tb16 = Er(nc.psum_tensor(pfx + "tb16", [16, 128], BF16))

        wld = b.sem(pfx + "wld")
        vg = b.sem(pfx + "vg")
        b.dma("sync", lb1[:, :], io["l3_lb1"][:, :], wld)
        b.dma("sync", hb1[:, :], io["l3_hb1"][:, :], wld)
        b.dma("gpsimd", lb2r[:, :], io["l3_lb2row"][:, :], wld)
        b.dma("gpsimd", hb2r[:, :], io["l3_hb2row"][:, :], wld)
        b.dma("sync", lg[:, :], io["l3_lg"][:, :], wld)
        b.dma("sync", hg[:, :], io["l3_hg"][:, :], wld)
        b.dma("gpsimd", rtc[:, :], io["l3_rt"][:, :], wld)

        pj = b.sem(pfx + "pj")
        pje = b.sem(pfx + "pje")
        it1 = b.sem(pfx + "it1")
        it2 = b.sem(pfx + "it2")
        trd = b.sem(pfx + "trd")
        dmv = b.sem(pfx + "dmv")
        mkd = b.sem(pfx + "mkd")
        gel = b.sem(pfx + "gel")
        htt = b.sem(pfx + "htt")
        oev = b.sem(pfx + "oev")
        ost = b.sem(pfx + "ost")
        wl2 = b.sem(pfx + "wl2")
        b.wait("tensor", cl)
        b.wait("tensor", wld)
        for bb2 in range(BLOC):
            c0 = bb2 * T
            for ch in range(4):
                b.wait("tensor", it1)
                for kt in range(DT):
                    b.op("tensor", lambda t, kt=kt, ch=ch, c0=c0: t.matmul(
                        nps[:, :], rtc[:, kt:kt + 1],
                        xn[:, kt * NTOK + c0 + ch * 512: kt * NTOK + c0 + ch * 512 + 512],
                        start=(kt == 0), stop=(kt == DT - 1)),
                        inc=(pj if kt == DT - 1 else None))
                b.wait("vector", pj)
                b.op("vector", lambda v, bb2=bb2, ch=ch: v.tensor_copy(
                    srow[bb2][0:1, ch * 512:(ch + 1) * 512], nps[:, :]), inc=it1)
            b.op("vector", lambda v, bb2=bb2: v.memset(sbrow[bb2][0:1, :], 0.0),
                 inc=it1)
            b.drain("vector")
        for it_ in range(50):
            for bb2 in range(BLOC):
                b.wait("scalar", it1)
                b.op("scalar", lambda s, bb2=bb2: s.activation(
                    out=escr[0:1, :], in_=sbrow[bb2][0:1, :], func=AF.Exp,
                    accum_out=tot[0:1, bb2:bb2 + 1]), inc=it2)
                b.drain("scalar")
                b.op("scalar", lambda s, bb2=bb2: s.activation(
                    out=tot[0:1, 2 + bb2:3 + bb2], in_=tot[0:1, bb2:bb2 + 1],
                    func=AF.Ln), inc=it2)
                b.drain("scalar")
            for bb2 in range(BLOC):
                b.wait("vector", it2)
                b.op("vector", lambda v, bb2=bb2: v.tensor_scalar(
                    out=tot[0:1, 2 + bb2:3 + bb2], in0=tot[0:1, 2 + bb2:3 + bb2],
                    scalar1=-LOGK, scalar2=None, op0=ALU.add))
                b.drain("vector")
                b.op("vector", lambda v, bb2=bb2: v.tensor_scalar(
                    out=sbrow[bb2][0:1, :], in0=srow[bb2][0:1, :],
                    scalar1=tot[0:1, 2 + bb2:3 + bb2], scalar2=None,
                    op0=ALU.min), inc=it1)
                b.drain("vector")
        for bb2 in range(BLOC):
            b.op("vector", lambda v, bb2=bb2: v.tensor_scalar(
                out=tot[0:1, 2 + bb2:3 + bb2], in0=tot[0:1, 2 + bb2:3 + bb2],
                scalar1=-1.0, scalar2=None, op0=ALU.mult))
            b.drain("vector")
            b.op("vector", lambda v, bb2=bb2: v.tensor_scalar(
                out=sbrow[bb2][0:1, :], in0=srow[bb2][0:1, :],
                scalar1=tot[0:1, 2 + bb2:3 + bb2], scalar2=0.0,
                op0=ALU.add, op1=ALU.min), inc=it1)
            b.drain("vector")
            b.wait("scalar", it1)
            b.op("scalar", lambda s, bb2=bb2: s.activation(
                out=sbrow[bb2][0:1, :], in_=sbrow[bb2][0:1, :], func=AF.Exp),
                inc=it2)
        for bb2 in range(BLOC):
            b.wait("tensor", it2)
            for ch in range(4):
                b.op("tensor", lambda t, bb2=bb2, ch=ch: t.matmul(
                    bps[:, :], hb["ones_row_f"][0:1, :],
                    sbrow[bb2][0:1, ch * 512:(ch + 1) * 512],
                    start=True, stop=True), inc=pj)
                b.wait("vector", pj)
                b.op("vector", lambda v, ch=ch: v.tensor_copy(
                    scb[:, ch * 512:(ch + 1) * 512], bps[:, :]), inc=pje)
                b.wait("tensor", pje)
            b.wait("sync", it2)
            b.dma("sync", stg16[:, :], sbrow[bb2][0:1, :], dmv)
            b.wait("tensor", dmv)
            b.op("tensor", lambda t: t.transpose(
                t16[:, :], stg16[:, :], hb["ident_f"][0:16, 0:16]), inc=trd)
            b.wait("vector", trd)
            b.op("vector", lambda v: v.tensor_copy(sctm[:, :],
                                                   t16[:, 0:16]), inc=trd)
            b.drain("vector")
            for qt in range(16):
                b.op("vector", lambda v, qt=qt: v.tensor_scalar(
                    out=junk[:, :], in0=scb[:, :], scalar1=sctm[:, qt:qt + 1],
                    scalar2=0.0, op0=ALU.is_gt, op1=ALU.add,
                    accum_out=cntc[:, qt:qt + 1]))
                b.drain("vector")
            b.op("vector", lambda v: v.tensor_scalar(
                out=satb[:, :], in0=sctm[:, :], scalar1=1.0, scalar2=None,
                op0=ALU.is_ge), inc=it1)
            b.drain("vector")
            b.wait("tensor", it1)
            b.op("tensor", lambda t: t.matmul(
                t16[:, :], hb["tri128s"][:, :], satb[:, :],
                start=True, stop=True), inc=pj)
            b.op("tensor", lambda t: t.matmul(
                nps[0:1, 0:16], hb["ones_col_bf"][:, 0:1], satb[:, :],
                start=True, stop=True), inc=pj)
            b.wait("vector", pj)
            b.op("vector", lambda v: v.tensor_copy(fine[:, :], t16[:, 0:16]))
            b.op("vector", lambda v: v.tensor_copy(rowa[0:1, 0:16], nps[0:1, 0:16]),
                 inc=it1)
            b.drain("vector")
            for i_, (stp, so, do) in enumerate(((1, 0, 16), (2, 16, 32),
                                                (4, 32, 16), (8, 16, 32))):
                src = rowa[0:1, so:so + 16]
                dst = rowa[0:1, do:do + 16]
                b.op("vector", lambda v, src=src, dst=dst: v.tensor_copy(dst, src))
                b.drain("vector")
                b.op("vector", lambda v, src=src, dst=dst, stp=stp: v.tensor_add(
                    dst[0:1, stp:16], dst[0:1, stp:16], src[0:1, 0:16 - stp]))
                b.drain("vector")
            b.op("vector", lambda v: v.tensor_sub(
                rowa[0:1, 48:64], rowa[0:1, 32:48], rowa[0:1, 0:16]), inc=it1)
            b.drain("vector")
            b.wait("tensor", it1)
            b.op("tensor", lambda t: t.matmul(
                bps[:, 0:16], hb["ones_row_f"][0:1, :], rowa[0:1, 48:64],
                start=True, stop=True), inc=pj)
            b.wait("vector", pj)
            b.op("vector", lambda v: v.tensor_copy(cbe[:, :], bps[:, 0:16]))
            b.drain("vector")
            b.op("vector", lambda v: v.tensor_add(fine[:, :], fine[:, :], cbe[:, :]))
            b.drain("vector")
            b.op("vector", lambda v: v.tensor_mul(fine[:, :], fine[:, :],
                                                  satb[:, :]))
            b.drain("vector")
            b.op("vector", lambda v: v.tensor_add(fine[:, :], fine[:, :], cntc[:, :]))
            b.drain("vector")
            b.op("vector", lambda v: v.tensor_scalar(
                out=satb[:, :], in0=fine[:, :], scalar1=1024.0, scalar2=None,
                op0=ALU.is_lt), inc=it1)
            b.drain("vector")
            b.wait("tensor", it1)
            b.op("tensor", lambda t: t.transpose(
                tb16[:, :], satb[:, :], hb["ident_bf"][:, :]), inc=trd)
            b.wait("vector", trd)
            b.op("vector", lambda v: v.tensor_copy(msk16[:, :], tb16[0:16, 0:128]),
                 inc=trd)
            b.drain("vector")
            b.wait("sync", trd)
            b.dma("sync", mrow[bb2][0:1, :], msk16[:, :], dmv)
        if L3DBG:
            b.wait("sync", it1)
            b.wait("sync", it2)
            dsem = b.sem(pfx + "dbg")
            d_sc = nc.dram_tensor("dbg_sc", [2, T], F32, kind="ExternalOutput").ap()
            d_sr = nc.dram_tensor("dbg_sr", [2, T], F32, kind="ExternalOutput").ap()
            d_mr = nc.dram_tensor("dbg_mr", [2, T], BF16, kind="ExternalOutput").ap()
            d_cn = nc.dram_tensor("dbg_cn", [128, 16], F32, kind="ExternalOutput").ap()
            b.wait("sync", dmv)
            for i2 in range(2):
                b.dma("sync", d_sc[i2:i2 + 1, :], sbrow[i2][0:1, :], dsem)
                b.dma("sync", d_sr[i2:i2 + 1, :], srow[i2][0:1, :], dsem)
                b.dma("sync", d_mr[i2:i2 + 1, :], mrow[i2][0:1, :], dsem)
            b.dma("sync", d_cn[:, :], fine[:, :], dsem)
            b.wait("sync", dsem)
        rscope.close()
        lw1 = E(nc.sbuf_tensor(pfx + "lw1", [128, DT * 512], BF16))
        b.wait("gpsimd", dmv)
        hw1 = E(nc.sbuf_tensor(pfx + "hw1", [128, DT * 4096], BF16))
        hw2c = [E(nc.sbuf_tensor(pfx + f"hw2c{i}", [128, 512], BF16)) for i in range(2)]
        xng = E(nc.sbuf_tensor(pfx + "xng", [128, DT * 512], BF16))
        hlight = E(nc.sbuf_tensor(pfx + "hl", [128, 4 * 512], BF16))
        Hh = E(nc.sbuf_tensor(pfx + "Hh", [128, 32 * 512], BF16))
        osb = E(nc.sbuf_tensor(pfx + "osb", [128, 2 * 512], F32))
        gps = E(nc.psum_tensor(pfx + "gps", [128, 512], F32))
        acc = [E(nc.psum_tensor(pfx + f"acc{i}", [128, 512], F32)) for i in range(4)]
        b.dma("gpsimd", lw1[:, :].rearrange("p (kt m) -> p kt m", m=512),
              io["l3_lw1"].rearrange("(kt p) m -> p kt m", p=128), wld)
        for hf in range(4):
            b.dma("gpsimd", hw1[:, hf * 8192:(hf + 1) * 8192].rearrange(
                "p (kt m) -> p kt m", m=4096),
                io["l3_hw1"].rearrange("(kt p) m -> p kt m", p=128)
                [:, hf * 2:(hf + 1) * 2, :], wld)
        b.wait("vector", wld)
        for kt in range(DT):
            b.op("vector", lambda v, kt=kt: v.tensor_scalar(
                out=lw1[:, kt * 512:(kt + 1) * 512],
                in0=lw1[:, kt * 512:(kt + 1) * 512],
                scalar1=lg[:, kt:kt + 1], scalar2=None, op0=ALU.mult))
            b.drain("vector")
            b.op("vector", lambda v, kt=kt: v.tensor_scalar(
                out=hw1[:, kt * 4096:(kt + 1) * 4096],
                in0=hw1[:, kt * 4096:(kt + 1) * 4096],
                scalar1=hg[:, kt:kt + 1], scalar2=None, op0=ALU.mult), inc=vg)
            b.drain("vector")
        h2c = b.sem(pfx + "h2c")
        h2c0 = h2c.count
        wl20 = wl2.count
        HW2G = [0]
        b.wait("tensor", vg)
        b.wait("tensor", dmv)
        for tc in range(8):
            bb2 = tc // 4
            c0 = bb2 * T
            cc = (tc % 4) * 512
            b.wait("tensor", htt)
            it10 = it1.count
            for kt in range(DT):
                xv = xn[:, kt * NTOK + c0 + cc: kt * NTOK + c0 + cc + 512]
                x2v = xng[:, kt * 512:(kt + 1) * 512]
                b.op("vector", lambda v, xv=xv, x2v=x2v: v.tensor_mul(x2v, xv, xv),
                     inc=it1)
                b.wait("tensor", it1, it10 + kt + 1)
                b.op("tensor", lambda t, kt=kt, x2v=x2v: t.matmul(
                    nps[:, :], hb["ones_col_bf"][:, 0:1], x2v,
                    start=(kt == 0), stop=(kt == DT - 1)),
                    inc=(pj if kt == DT - 1 else None))
            b.wait("vector", pj)
            b.op("vector", lambda v: v.tensor_scalar(
                out=nrow[0:1, :], in0=nps[:, :], scalar1=1e-24, scalar2=None,
                op0=ALU.add), inc=it1)
            b.drain("vector")
            b.wait("scalar", it1)
            b.op("scalar", lambda s: s.activation(
                out=nrow[0:1, :], in_=nrow[0:1, :], func=AF.Ln))
            b.drain("scalar")
            b.op("scalar", lambda s: s.activation(
                out=nrow[0:1, :], in_=nrow[0:1, :], func=AF.Exp, scale=-0.5),
                inc=it2)
            b.wait("vector", it2)
            b.op("vector", lambda v: v.tensor_scalar(
                out=rnrow[0:1, :], in0=nrow[0:1, :], scalar1=32.0, scalar2=None,
                op0=ALU.mult), inc=it1)
            b.drain("vector")
            b.wait("tensor", it1)
            b.op("tensor", lambda t: t.matmul(
                bps[:, :], hb["ones_row_bf"][0:1, :], rnrow[0:1, :],
                start=True, stop=True), inc=pj)
            b.op("tensor", lambda t, bb2=bb2, cc=cc: t.matmul(
                gps[:, :], hb["ones_row_bf"][0:1, :],
                mrow[bb2][0:1, cc:cc + 512], start=True, stop=True), inc=pj)
            b.wait("vector", pj)
            b.op("vector", lambda v: v.tensor_copy(rnb[:, :], bps[:, :]))
            b.op("vector", lambda v: v.tensor_copy(maskb[:, :], gps[:, :]), inc=pje)
            b.drain("vector")
            for kt in range(DT):
                xv = xn[:, kt * NTOK + c0 + cc: kt * NTOK + c0 + cc + 512]
                b.op("vector", lambda v, kt=kt, xv=xv: v.tensor_mul(
                    xng[:, kt * 512:(kt + 1) * 512], xv, rnb[:, :]),
                    inc=(it1 if kt == DT - 1 else None))
            b.drain("vector")
            b.wait("tensor", it1)
            for lm in range(4):
                b.wait("tensor", gel)
                for kt in range(DT):
                    b.op("tensor", lambda t, kt=kt, lm=lm: t.matmul(
                        gps[:, :], lw1[:, kt * 512 + lm * 128: kt * 512 + lm * 128 + 128],
                        xng[:, kt * 512:(kt + 1) * 512],
                        start=(kt == 0), stop=(kt == DT - 1)),
                        inc=(pj if kt == DT - 1 else None))
                b.wait("scalar", pj)
                b.op("scalar", lambda s, lm=lm: s.activation(
                    out=hlight[:, lm * 512:(lm + 1) * 512], in_=gps[:, :],
                    func=AF.Gelu, bias=lb1[:, lm:lm + 1]), inc=gel)
            for hm in range(32):
                b.wait("tensor", gel)
                for kt in range(DT):
                    b.op("tensor", lambda t, kt=kt, hm=hm: t.matmul(
                        gps[:, :], hw1[:, kt * 4096 + hm * 128: kt * 4096 + hm * 128 + 128],
                        xng[:, kt * 512:(kt + 1) * 512],
                        start=(kt == 0), stop=(kt == DT - 1)),
                        inc=(pj if kt == DT - 1 else None))
                b.wait("scalar", pj)
                b.op("scalar", lambda s, hm=hm: s.activation(
                    out=Hh[:, hm * 512:(hm + 1) * 512], in_=gps[:, :],
                    func=AF.Gelu, bias=hb1[:, hm:hm + 1]), inc=gel)
                b.wait("vector", gel)
                b.op("vector", lambda v, hm=hm: v.tensor_mul(
                    Hh[:, hm * 512:(hm + 1) * 512], Hh[:, hm * 512:(hm + 1) * 512],
                    maskb[:, :]), inc=htt)
                b.drain("vector")
            b.wait("tensor", htt)
            if L3DBG and tc == 0:
                d_w1 = nc.dram_tensor("dbg_w1", [128, DT * 512], BF16,
                                      kind="ExternalOutput").ap()
                dsw = b.sem(pfx + "dbgw")
                b.wait("sync", vg)
                b.dma("sync", d_w1[:, :], lw1[:, :], dsw)
                b.wait("sync", dsw)
                b.wait("sync", it1)
                b.wait("sync", gel)
                b.wait("sync", htt)
                dsem2 = b.sem(pfx + "dbg2")
                d_rn = nc.dram_tensor("dbg_rn", [1, 512], BF16,
                                      kind="ExternalOutput").ap()
                d_hl = nc.dram_tensor("dbg_hl", [128, 512], BF16,
                                      kind="ExternalOutput").ap()
                d_hh = nc.dram_tensor("dbg_hh", [128, 512], BF16,
                                      kind="ExternalOutput").ap()
                d_xg = nc.dram_tensor("dbg_xg", [128, 512], BF16,
                                      kind="ExternalOutput").ap()
                b.dma("sync", d_rn[0:1, :], rnrow[0:1, :], dsem2)
                b.dma("sync", d_hl[:, :], hlight[:, 0:512], dsem2)
                b.dma("sync", d_hh[:, :], Hh[:, 0:512], dsem2)
                b.dma("sync", d_xg[:, :], xng[:, 0:512], dsem2)
                b.wait("sync", dsem2)
            for og in range(2):
                b.wait("tensor", oev)
                for odt in range(4):
                    o = og * 4 + odt
                    first = True
                    for j in range(36):
                        g = HW2G[0]
                        sl = g % 2
                        if g == 0:
                            b.dma("gpsimd", hw2c[0][:, 0:128],
                                  io["l3_lw2"][0:128, 0:128], wl2)
                        gn = g + 1
                        if gn < 8 * 8 * 36:
                            jn = gn % 36
                            on = (gn // 36) % 8
                            if jn < 4:
                                nsrc = io["l3_lw2"][jn * 128:(jn + 1) * 128,
                                                    on * 128:(on + 1) * 128]
                            else:
                                nsrc = io["l3_hw2"][(jn - 4) * 128:(jn - 3) * 128,
                                                    on * 128:(on + 1) * 128]
                            if g >= 1:
                                b.wait("gpsimd", h2c, h2c0 + g)
                            b.dma("gpsimd", hw2c[gn % 2][:, 0:128], nsrc, wl2)
                        b.wait("tensor", wl2, wl20 + 16 * (g + 1))
                        if j < 4:
                            rh = hlight[:, j * 512:(j + 1) * 512]
                        else:
                            rh = Hh[:, (j - 4) * 512:(j - 3) * 512]
                        b.op("tensor", lambda t, odt=odt, sl=sl, rh=rh, first=first:
                             t.matmul(acc[odt][:, :], hw2c[sl][:, 0:128], rh,
                                      start=first, stop=False,
                                      skip_group_check=True), inc=h2c)
                        first = False
                        HW2G[0] += 1
                    b.op("tensor", lambda t, o=o, odt=odt: t.matmul(
                        acc[odt][:, :], lb2r[0:1, o * 128:(o + 1) * 128],
                        hb["ones512"][0:1, :],
                        start=False, stop=False, skip_group_check=True))
                    b.op("tensor", lambda t, o=o, odt=odt, bb2=bb2, cc=cc: t.matmul(
                        acc[odt][:, :], hb2r[0:1, o * 128:(o + 1) * 128],
                        mrow[bb2][0:1, cc:cc + 512],
                        start=False, stop=True, skip_group_check=True), inc=pj)
                for odt in range(4):
                    o = og * 4 + odt
                    sl = odt % 2
                    b.wait("vector", pj)
                    b.wait("vector", h2c)
                    if ost.count >= 32:
                        b.wait("vector", ost, ost.count - 16)
                    b.op("vector", lambda v, odt=odt, sl=sl: v.tensor_copy(
                        osb[:, sl * 512:(sl + 1) * 512], acc[odt][:, :]), inc=oev)
                    b.drain("vector")
                    if L3DBG and tc == 0 and og == 0 and odt == 0:
                        dsem3 = b.sem(pfx + "dbg3")
                        d_ac = nc.dram_tensor("dbg_ac", [128, 512], F32,
                                              kind="ExternalOutput").ap()
                        b.wait("sync", oev)
                        b.dma("sync", d_ac[:, :], osb[:, 0:512], dsem3)
                        b.wait("sync", dsem3)
                    b.wait("gpsimd", oev)
                    b.dma("gpsimd",
                          g_raw[o * 128:(o + 1) * 128, c0 + cc: c0 + cc + 512],
                          osb[:, sl * 512:(sl + 1) * 512], ost)
        b.wait("gpsimd", ost)
        b.barrier()
    b.sem_release(_sm)



def build_ln_sb(b, xbuf, lnw, lnb, pfx):
    nc, hb = b.nc, b.hb
    _sm = b.sem_mark()
    NCH = NTOK // 512
    with ExitStack() as sc:
        E = sc.enter_context
        x2 = E(nc.sbuf_tensor(pfx + "x2", [128, 2 * 512], BF16))
        rows = E(nc.sbuf_tensor(pfx + "rows", [1, 4 * 512], F32))
        abrow = E(nc.sbuf_tensor(pfx + "abrow", [1, 2 * 512], F32))
        abb = E(nc.sbuf_tensor(pfx + "abb", [128, 2 * NTOK], BF16))
        tmp = E(nc.sbuf_tensor(pfx + "tmp", [128, 512], BF16))
        ps1 = E(nc.psum_tensor(pfx + "ps1", [1, 512], F32))
        ps2 = E(nc.psum_tensor(pfx + "ps2", [1, 512], F32))
        psb = [E(nc.psum_tensor(pfx + f"psb{i}", [128, 512], F32)) for i in range(2)]
        vs = b.sem(pfx + "v")
        mm = b.sem(pfx + "mm")
        mm0 = mm.count
        ev = b.sem(pfx + "ev")
        ssem = b.sem(pfx + "sc")
        vr = b.sem(pfx + "vr")
        bmm = b.sem(pfx + "bmm")
        bev = b.sem(pfx + "bev")
        step = 0
        for ch in range(NCH):
            for dt in range(DT):
                xv = xbuf[:, dt * NTOK + ch * 512: dt * NTOK + ch * 512 + 512]
                x2v = x2[:, (step % 2) * 512:(step % 2) * 512 + 512]
                if step >= 2:
                    b.wait("vector", mm, mm0 + step - 1)
                b.op("vector", lambda v, xv=xv, x2v=x2v: v.tensor_mul(x2v, xv, xv),
                     inc=vs)
                b.wait("tensor", vs)
                if dt == 0:
                    b.wait("tensor", ev)
                b.op("tensor", lambda t, xv=xv, dt=dt: t.matmul(
                    ps1[:, :], hb["ones_col_bf"][:, 0:1], xv,
                    start=(dt == 0), stop=(dt == DT - 1)))
                b.op("tensor", lambda t, x2v=x2v, dt=dt: t.matmul(
                    ps2[:, :], hb["ones_col_bf"][:, 0:1], x2v,
                    start=(dt == 0), stop=(dt == DT - 1)), inc=mm)
                step += 1
            s1 = rows[:, 0:512]
            s2 = rows[:, 512:1024]
            sc1 = rows[:, 1024:1536]
            sc2 = rows[:, 1536:2048]
            b.wait("vector", mm)
            b.wait("vector", ssem)
            b.op("vector", lambda v: v.tensor_copy(s1, ps1[:, :]))
            b.op("vector", lambda v: v.tensor_copy(s2, ps2[:, :]), inc=ev)
            b.drain("vector")
            b.op("vector", lambda v: v.tensor_scalar(
                out=sc1, in0=s1, scalar1=1.0 / D, scalar2=None, op0=ALU.mult))
            b.op("vector", lambda v: v.tensor_scalar(
                out=s2, in0=s2, scalar1=1.0 / D, scalar2=1e-5, op0=ALU.mult,
                op1=ALU.add))
            b.drain("vector")
            b.op("vector", lambda v: v.tensor_mul(sc2, sc1, sc1))
            b.drain("vector")
            b.op("vector", lambda v: v.tensor_sub(s2, s2, sc2), inc=vr)
            b.wait("scalar", vr)
            b.op("scalar", lambda s: s.activation(out=sc2, in_=s2, func=AF.Ln))
            b.drain("scalar")
            b.op("scalar", lambda s: s.activation(
                out=s1, in_=sc2, func=AF.Exp, scale=-0.5), inc=ssem)
            b.wait("vector", ssem)
            b.op("vector", lambda v: v.tensor_copy(abrow[:, 0:512], s1))
            b.op("vector", lambda v: v.scalar_tensor_tensor(
                out=abrow[:, 512:1024], in0=sc1, scalar=-1.0, in1=s1,
                op0=ALU.mult, op1=ALU.mult), inc=ssem)
            b.drain("vector")
            b.wait("tensor", ssem)
            for i in range(2):
                bank = psb[i]
                if bev.count >= 2:
                    b.wait("tensor", bev, bev.count - 1)
                b.op("tensor", lambda t, i=i, bank=bank: t.matmul(
                    bank[:, :], hb["ones_row_f"][0:1, :],
                    abrow[0:1, i * 512:(i + 1) * 512], start=True, stop=True),
                    inc=bmm)
                b.wait("vector", bmm)
                b.op("vector", lambda v, i=i, ch=ch, bank=bank: v.tensor_copy(
                    abb[:, i * NTOK + ch * 512: i * NTOK + ch * 512 + 512],
                    bank[:, :]), inc=bev)
        xnv = b.sem(pfx + "xnv")
        b.wait("vector", bev)
        for dt in range(DT):
            for ch in range(NCH):
                xv = xbuf[:, dt * NTOK + ch * 512: dt * NTOK + ch * 512 + 512]
                Av = abb[:, ch * 512:(ch + 1) * 512]
                Bv = abb[:, NTOK + ch * 512: NTOK + (ch + 1) * 512]
                b.op("vector", lambda v, xv=xv, Av=Av: v.tensor_mul(tmp[:, :], xv, Av))
                b.drain("vector")
                b.op("vector", lambda v, xv=xv, Bv=Bv: v.tensor_add(
                    xv, tmp[:, :], Bv))
                b.drain("vector")
                b.op("vector", lambda v, xv=xv, dt=dt: v.tensor_scalar(
                    out=xv, in0=xv, scalar1=lnw[:, dt:dt + 1],
                    scalar2=lnb[:, dt:dt + 1], op0=ALU.mult, op1=ALU.add),
                    inc=xnv)
                b.drain("vector")
    b.sem_release(_sm)
    return xnv


def build_l3_attn(b, xn, g_raw, x_in, x_out, io, pfx="a3"):
    nc, hb = b.nc, b.hb
    _sm = b.sem_mark()
    with ExitStack() as sc:
        E = sc.enter_context
        wsh = E(nc.sbuf_tensor(pfx + "wsh", [128, DT * D], BF16))
        sgt = E(nc.sbuf_tensor(pfx + "sgt", [128, 24], F32))
        lnw = E(nc.sbuf_tensor(pfx + "lnw", [128, 8], F32))
        lnb = E(nc.sbuf_tensor(pfx + "lnb", [128, 8], F32))
        wkvT = E(nc.sbuf_tensor(pfx + "wkvT", [128, DT * 128], BF16))
        tdr = E(nc.sbuf_tensor(pfx + "tdr", [1, 64], F32))
        tfr = E(nc.sbuf_tensor(pfx + "tfr", [1, 64], F32))
        woutT = E(nc.sbuf_tensor(pfx + "woutT", [64, D], BF16))
        stag = E(nc.sbuf_tensor(pfx + "stag", [128, 2 * DT * 512], BF16))
        iotj = E(nc.sbuf_tensor(pfx + "iotj", [128, 64], F32))
        tdb = E(nc.sbuf_tensor(pfx + "tdb", [128, 64], F32))
        etfb = E(nc.sbuf_tensor(pfx + "etfb", [128, 64], BF16))
        etfr = E(nc.sbuf_tensor(pfx + "etfr", [1, 64], F32))
        invd = E(nc.sbuf_tensor(pfx + "invd", [128, 64], BF16))
        decw = E(nc.sbuf_tensor(pfx + "decw", [128, 64], BF16))
        wrows = E(nc.sbuf_tensor(pfx + "wrows", [1, 3 * 64], F32))
        srw = E(nc.sbuf_tensor(pfx + "srw", [1, 3 * 64], F32))
        srb = E(nc.sbuf_tensor(pfx + "srb", [1, 64], BF16))
        ksig = E(nc.sbuf_tensor(pfx + "ksig", [128, 64], BF16))
        e1b = E(nc.sbuf_tensor(pfx + "e1b", [128, 64], BF16))
        wkvb = E(nc.sbuf_tensor(pfx + "wkvb", [128, 16 * 64], BF16))
        ubuf = E(nc.sbuf_tensor(pfx + "ubuf", [128, 64], BF16))
        tmp2 = E(nc.sbuf_tensor(pfx + "tmp2", [64, 64], F32))
        sttm = E(nc.sbuf_tensor(pfx + "sttm", [64, 64], BF16))
        stfm = E(nc.sbuf_tensor(pfx + "stfm", [64, NTOK], BF16))
        gch = E(nc.sbuf_tensor(pfx + "gch", [128, 512], F32))
        xch = E(nc.sbuf_tensor(pfx + "xch", [128, 512], F32))
        xch2 = E(nc.sbuf_tensor(pfx + "xch2", [128, 512], F32))
        prp = E(nc.psum_tensor(pfx + "prp", [128, 512], F32))

        wld = b.sem(pfx + "wld")
        b.dma("gpsimd", wsh[:, :].rearrange("p (kt m) -> p kt m", m=D),
              io["l3_wshT"].rearrange("(kt p) m -> p kt m", p=128), wld)
        b.dma("sync", sgt[:, 0:8], io["l3_sgate"][:, :], wld)
        b.dma("sync", lnw[:, :], io["l3_lnw"][:, :], wld)
        b.dma("sync", lnb[:, :], io["l3_lnb"][:, :], wld)
        b.dma("gpsimd", wkvT[:, :].rearrange("p (kt m) -> p kt m", m=128),
              io["l3_wkvT"].rearrange("(kt p) m -> p kt m", p=128), wld)
        b.dma("sync", tdr[:, :], io["l3_td"][:, :], wld)
        b.dma("sync", tfr[:, :], io["l3_tf"][:, :], wld)
        b.dma("gpsimd", woutT[:, :], io["l3_woutT"][:, :], wld)

        pj = b.sem(pfx + "pj")
        pje = b.sem(pfx + "pje")
        vx = b.sem(pfx + "vx")
        sgd = b.sem(pfx + "sgd")
        cmb = b.sem(pfx + "cmb")
        cld = b.sem(pfx + "cld")
        ostx = b.sem(pfx + "ostx")
        scn = b.sem(pfx + "scn")
        trd = b.sem(pfx + "trd")
        kvd = b.sem(pfx + "kvd")

        b.wait("scalar", wld)
        b.op("scalar", lambda s: s.activation(
            out=sgt[:, 8:16], in_=sgt[:, 0:8], func=AF.Sigmoid), inc=sgd)
        b.wait("vector", sgd)
        b.op("vector", lambda v: v.tensor_scalar(
            out=sgt[:, 16:24], in0=sgt[:, 8:16], scalar1=-1.0, scalar2=1.0,
            op0=ALU.mult, op1=ALU.add), inc=vx)
        b.drain("vector")
        b.wait("tensor", wld)
        b.wait("tensor", vx)
        for bb2 in range(BLOC):
            c0 = bb2 * T
            for chp in range(2):
                chA, chB = chp, chp + 2
                for half, ch, rch in ((0, chA, chB), (1, chB, chA)):
                    for mt in range(DT):
                        b.wait("tensor", pje)
                        for kt in range(DT):
                            b.op("tensor", lambda t, kt=kt, mt=mt, rch=rch, c0=c0:
                                 t.matmul(
                                     prp[:, :],
                                     wsh[:, kt * D + mt * 128: kt * D + mt * 128 + 128],
                                     xn[:, kt * NTOK + c0 + rch * 512: kt * NTOK + c0 + rch * 512 + 512],
                                     start=(kt == 0), stop=(kt == DT - 1)),
                                 inc=(pj if kt == DT - 1 else None))
                        b.wait("vector", pj)
                        dst = stag[:, (half * DT + mt) * 512:(half * DT + mt) * 512 + 512]
                        xv = xn[:, mt * NTOK + c0 + ch * 512: mt * NTOK + c0 + ch * 512 + 512]
                        b.op("vector", lambda v, dst=dst, xv=xv, mt=mt: v.tensor_scalar(
                            out=dst, in0=xv, scalar1=sgt[:, 16 + mt:17 + mt],
                            scalar2=None, op0=ALU.mult))
                        b.drain("vector")
                        b.op("vector", lambda v, dst=dst, mt=mt: v.scalar_tensor_tensor(
                            out=dst, in0=prp[:, :], scalar=sgt[:, 8 + mt:9 + mt],
                            in1=dst, op0=ALU.mult, op1=ALU.add), inc=pje)
                        b.drain("vector")
                b.wait("vector", pje)
                for half, ch in ((0, chA), (1, chB)):
                    for mt in range(DT):
                        src = stag[:, (half * DT + mt) * 512:(half * DT + mt) * 512 + 512]
                        b.op("vector", lambda v, src=src, mt=mt, ch=ch, c0=c0:
                             v.tensor_copy(
                                 xn[:, mt * NTOK + c0 + ch * 512: mt * NTOK + c0 + ch * 512 + 512],
                                 src), inc=vx)
                b.drain("vector")
                b.wait("tensor", vx)
        b.barrier()
        s_ln = build_ln_sb(b, xn, lnw, lnb, pfx + "iln")
        for e in ENGINES:
            b.wait(e, s_ln)
        b.barrier()
        kvp = E(nc.psum_tensor(pfx + "kvp", [128, 128], F32))
        scp = E(nc.psum_tensor(pfx + "scp", [64, 64], F32))
        bp2 = E(nc.psum_tensor(pfx + "bp2", [1, 64], F32))
        tpp = E(nc.psum_tensor(pfx + "tpp", [64, 64], BF16))
        bcp = E(nc.psum_tensor(pfx + "bcp", [128, 64], F32))
        b.op("gpsimd", lambda g: g.iota(iotj[:, :], pattern=[[0, 64]], base=0,
                                        channel_multiplier=1,
                                        allow_small_or_imprecise_dtypes=True),
             inc=kvd)
        b.wait("vector", kvd)
        b.op("vector", lambda v: v.tensor_scalar(
            out=iotj[64:128, :], in0=iotj[64:128, :], scalar1=-64.0, scalar2=None,
            op0=ALU.add), inc=kvd)
        b.drain("vector")
        b.wait("tensor", kvd)
        b.op("tensor", lambda t: t.matmul(
            bcp[:, 0:64], hb["ones_row_f"][0:1, :], tdr[0:1, :],
            start=True, stop=True), inc=pj)
        b.wait("vector", pj)
        b.op("vector", lambda v: v.tensor_copy(tdb[:, :], bcp[:, 0:64]), inc=kvd)
        b.drain("vector")
        b.op("vector", lambda v: v.tensor_mul(iotj[:, :], iotj[:, :], tdb[:, :]),
             inc=kvd)
        b.drain("vector")
        b.op("vector", lambda v: v.tensor_add(tdb[:, :], iotj[:, :], tdb[:, :]),
             inc=kvd)
        b.drain("vector")
        b.wait("scalar", kvd)
        b.op("scalar", lambda s: s.activation(out=invd[:, :], in_=iotj[:, :],
                                              func=AF.Exp, scale=-1.0))
        b.op("scalar", lambda s: s.activation(out=decw[:, :], in_=tdb[:, :],
                                              func=AF.Exp))
        b.op("scalar", lambda s: s.activation(out=wrows[0:1, 0:64], in_=tdr[0:1, :],
                                              func=AF.Exp))
        b.op("scalar", lambda s: s.activation(out=wrows[0:1, 64:128], in_=tdr[0:1, :],
                                              func=AF.Exp, scale=63.0))
        b.op("scalar", lambda s: s.activation(out=wrows[0:1, 128:192], in_=tdr[0:1, :],
                                              func=AF.Exp, scale=64.0))
        b.op("scalar", lambda s: s.activation(out=etfr[0:1, :], in_=tfr[0:1, :],
                                              func=AF.Exp), inc=sgd)
        b.wait("tensor", sgd)
        b.op("tensor", lambda t: t.matmul(
            bcp[:, 0:64], hb["ones_row_f"][0:1, :], etfr[0:1, :],
            start=True, stop=True), inc=pj)
        b.wait("vector", pj)
        b.op("vector", lambda v: v.tensor_copy(etfb[:, :], bcp[:, 0:64]), inc=kvd)
        b.drain("vector")
        for bb2 in range(BLOC):
            c0 = bb2 * T
            b.wait("tensor", kvd)
            for tt2 in range(16):
                b.wait("tensor", vx)
                for kt in range(DT):
                    b.op("tensor", lambda t, kt=kt, tt2=tt2, c0=c0: t.matmul(
                        kvp[:, :],
                        xn[:, kt * NTOK + c0 + tt2 * 128: kt * NTOK + c0 + tt2 * 128 + 128],
                        wkvT[:, kt * 128:(kt + 1) * 128],
                        start=(kt == 0), stop=(kt == DT - 1)),
                        inc=(pj if kt == DT - 1 else None))
                b.wait("scalar", pj)
                b.wait("scalar", vx)
                b.op("scalar", lambda s: s.activation(
                    out=ksig[:, :], in_=kvp[:, 0:64], func=AF.Sigmoid), inc=sgd)
                b.wait("vector", sgd)
                b.op("vector", lambda v: v.tensor_mul(ksig[:, :], ksig[:, :],
                                                      etfb[:, :]), inc=vx)
                b.drain("vector")
                b.wait("scalar", vx)
                b.op("scalar", lambda s: s.activation(
                    out=e1b[:, :], in_=ksig[:, :], func=AF.Exp, scale=-1.0),
                    inc=sgd)
                b.wait("vector", sgd)
                b.op("vector", lambda v, tt2=tt2: v.tensor_mul(
                    wkvb[:, tt2 * 64:(tt2 + 1) * 64], e1b[:, :], kvp[:, 64:128]),
                    inc=vx)
                b.drain("vector")
            b.op("vector", lambda v: v.memset(srw[0:1, 0:64], 0.0), inc=scn)
            b.drain("vector")
            for c in range(32):
                par = c % 2
                p0 = par * 64
                usrc = wkvb[p0:p0 + 64, (c // 2) * 64:(c // 2) * 64 + 64]
                b.wait("vector", vx)
                b.wait("vector", trd)
                b.op("vector", lambda v, usrc=usrc, p0=p0: v.tensor_mul(
                    ubuf[p0:p0 + 64, :], usrc, invd[p0:p0 + 64, :]))
                b.drain("vector")
                b.op("vector", lambda v: v.tensor_mul(
                    srb[0:1, :], srw[0:1, 0:64], wrows[0:1, 0:64]), inc=scn)
                b.drain("vector")
                b.wait("tensor", scn)
                b.op("tensor", lambda t, p0=p0: t.matmul(
                    scp[:, :], hb["tri64"][p0:p0 + 64, :], ubuf[p0:p0 + 64, :],
                    start=True, stop=True, tile_position=(p0, 0)))
                b.op("tensor", lambda t: t.matmul(
                    bcp[0:64, 0:64], hb["ones_row_bf"][0:1, 0:64], srb[0:1, :],
                    start=True, stop=True), inc=pj)
                b.op("tensor", lambda t, p0=p0: t.matmul(
                    bp2[:, :], hb["ones_col_bf"][p0:p0 + 64, 0:1],
                    ubuf[p0:p0 + 64, :],
                    start=True, stop=True, tile_position=(p0, 0)), inc=pj)
                b.wait("vector", pj)
                b.op("vector", lambda v: v.tensor_mul(
                    tmp2[:, :], bcp[0:64, 0:64], decw[0:64, :]))
                b.drain("vector")
                b.op("vector", lambda v: v.tensor_mul(
                    sttm[:, :], scp[:, :], decw[0:64, :]))
                b.drain("vector")
                b.op("vector", lambda v: v.tensor_add(
                    sttm[:, :], sttm[:, :], tmp2[:, :]), inc=trd)
                b.drain("vector")
                b.op("vector", lambda v: v.tensor_mul(
                    srw[0:1, 128:192], srw[0:1, 0:64], wrows[0:1, 128:192]))
                b.drain("vector")
                b.op("vector", lambda v: v.tensor_mul(
                    srw[0:1, 64:128], bp2[0:1, :], wrows[0:1, 64:128]))
                b.drain("vector")
                b.op("vector", lambda v: v.tensor_add(
                    srw[0:1, 0:64], srw[0:1, 64:128], srw[0:1, 128:192]), inc=scn)
                b.drain("vector")
                b.wait("tensor", trd)
                b.op("tensor", lambda t: t.transpose(
                    tpp[:, :], sttm[:, :], hb["ident_bf"][0:64, 0:64]), inc=pj)
                b.wait("vector", pj)
                b.op("vector", lambda v, c=c, c0=c0: v.tensor_copy(
                    stfm[0:64, c0 + c * 64: c0 + c * 64 + 64],
                    tpp[0:64, :]), inc=trd)
                b.drain("vector")
            b.wait("tensor", trd)
            for ch in range(4):
                for odt in range(DT):
                    b.wait("sync", cmb)
                    b.wait("sync", sgd)
                    b.dma("sync", gch[:, :],
                          g_raw[odt * 128:(odt + 1) * 128, c0 + ch * 512: c0 + ch * 512 + 512],
                          cld)
                    b.dma("sync", xch2[:, :],
                          x_in[odt * 128:(odt + 1) * 128, c0 + ch * 512: c0 + ch * 512 + 512],
                          cld)
                    b.wait("tensor", cmb)
                    b.op("tensor", lambda t, odt=odt, ch=ch, c0=c0: t.matmul(
                        prp[:, :], woutT[0:64, odt * 128:(odt + 1) * 128],
                        stfm[0:64, c0 + ch * 512: c0 + ch * 512 + 512],
                        start=True, stop=True), inc=pj)
                    b.wait("scalar", cld)
                    b.op("scalar", lambda s: s.activation(
                        out=gch[:, :], in_=gch[:, :], func=AF.Silu), inc=sgd)
                    b.wait("vector", pj)
                    b.wait("vector", sgd)
                    b.op("vector", lambda v: v.tensor_mul(
                        xch[:, :], prp[:, :], gch[:, :]), inc=cmb)
                    b.drain("vector")
                    b.op("vector", lambda v: v.tensor_add(
                        gch[:, :], xch[:, :], xch2[:, :]), inc=cmb)
                    b.drain("vector")
                    b.wait("sync", cmb)
                    b.dma("sync",
                          x_out[odt * 128:(odt + 1) * 128, c0 + ch * 512: c0 + ch * 512 + 512],
                          gch[:, :], ostx)
        b.wait("sync", ostx)
    b.sem_release(_sm)
    return ostx


def build_l3(b, xn, g_raw, x_in, x_out, io):
    cl3 = b.sem("l3cl")
    for name in L3_CONSTS:
        h = b.alloc("l3c_" + name, list(io[name].shape), CONST_DTYPES[name])
        b.hb[name] = h
        b.dma("sync", h[:, :], io[name][:, :], cl3)
    for e in ENGINES:
        b.wait(e, cl3)
    s_ln = build_ln(b, xn, x_in, "ln3")
    for e in ENGINES:
        b.wait(e, s_ln)
    b.barrier()
    build_l3_ffn(b, xn, g_raw, io)
    b.barrier()
    build_l3_attn(b, xn, g_raw, x_in, x_out, io)
    b.barrier()



def build_l1(b, xn, g_raw, scr, x_in, x_out, io, upto="all"):
    s_ln = build_ln(b, xn, x_in, "ln1")
    for e in ENGINES:
        b.wait(e, s_ln)
    b.barrier()
    if upto == "ln":
        return
    if upto.startswith("ffn"):
        build_l1_ffn(b, xn, g_raw, scr, io, upto=upto[3:] or "all")
        b.barrier()
        return
    build_l1_ffn(b, xn, g_raw, scr, io)
    b.barrier()
    build_l1_attn(b, xn, g_raw, x_in, x_out, io,
                  upto=(upto[2:] if upto.startswith("at") else "all"))
    b.barrier()


def build_l0_program(debug_taps=(), stages="all", nlayers=1, dbg_x=False):
    nc = bass.Bass("TRN2", target_bir_lowering=False)
    ctx = ExitStack()
    io = {}
    specs = dict(INPUT_SPECS_L0)
    if nlayers >= 2:
        specs.update(INPUT_SPECS_L1)
    if nlayers >= 3:
        specs.update(INPUT_SPECS_L2)
    if nlayers >= 4:
        specs.update(INPUT_SPECS_L3)
    for name, (shape, dt_) in specs.items():
        io[name] = nc.dram_tensor(name, shape, dt_, kind="ExternalInput").ap()
    for name, dt_ in CONST_DTYPES.items():
        shape = list(make_constants()[name].shape)
        io[name] = nc.dram_tensor(name, shape, dt_, kind="ExternalInput").ap()
    x1 = nc.dram_tensor("x1", [D, NTOK], F32,
                        kind=("ExternalOutput" if (nlayers == 1 or dbg_x) else "Internal")).ap()
    if nlayers >= 2:
        x2 = nc.dram_tensor("x2", [D, NTOK], F32,
                            kind=("ExternalOutput" if (nlayers == 2 or dbg_x) else "Internal")).ap()
    if nlayers >= 3:
        x3 = nc.dram_tensor("x3", [D, NTOK], F32,
                            kind=("ExternalOutput" if (nlayers == 3 or dbg_x) else "Internal")).ap()
    if nlayers >= 4:
        x4 = nc.dram_tensor("x4", [D, NTOK], F32, kind="ExternalOutput").ap()
    g_raw = nc.dram_tensor("g_raw", [D, NTOK], F32, kind="Internal").ap()
    wg_hbm = nc.dram_tensor("wg_scr", [8, NTOK], BF16, kind="Internal").ap()
    scr = {}
    if nlayers >= 2:
        scr["xq"] = nc.dram_tensor("xq_scr", [NTOK, D], BF16, kind="Internal").ap()
        scr["hq"] = nc.dram_tensor("hq_scr", [NTOK, 2816], BF16, kind="Internal").ap()
        scr["wgq"] = nc.dram_tensor("wgq_scr", [D, 2816], BF16, kind="Internal").ap()
        scr["wuq"] = nc.dram_tensor("wuq_scr", [D, 2816], BF16, kind="Internal").ap()
        scr["wdq"] = nc.dram_tensor("wdq_scr", [2816, D], BF16, kind="Internal").ap()
    taps = {}
    for tp in debug_taps:
        if tp == "xn":
            taps["xn"] = nc.dram_tensor("tap_xn", [128, DT * NTOK], F32,
                                        kind="ExternalOutput").ap()
        if tp == "g_raw":
            taps["g_raw"] = nc.dram_tensor("tap_g", [D, NTOK], F32,
                                           kind="ExternalOutput").ap()
        if tp == "g1":
            taps["g1"] = nc.dram_tensor("tap_g1", [D, NTOK], F32,
                                        kind="ExternalOutput").ap()

    b = B(nc, ctx)
    with ctx:
        cl = b.sem("cl")
        scoped = set(ATTN_CONSTS) | set(L1_CONSTS) | set(L3_CONSTS)
        for name, dt_ in CONST_DTYPES.items():
            if name in scoped:
                continue
            shape = list(make_constants()[name].shape)
            h = b.alloc("c_" + name, shape, dt_)
            b.hb[name] = h
            b.dma("sync", h[:, :], io[name][:, :], cl)
        for e in ENGINES:
            b.wait(e, cl)
        xn = b.alloc("xn", [128, DT * NTOK], BF16)
        s_ln = build_ln(b, xn, io["x0"], "ln0")
        for e in ENGINES:
            b.wait(e, s_ln)
        b.barrier()
        if stages in ("moe", "all"):
            s_moe = build_l0_moe(b, xn, g_raw, wg_hbm, io)
            b.barrier()
        if stages in ("lat", "av"):
            zs = b.sem("zs")
            b.dma("sync", g_raw[:, :], io["x0"][:, :], zs)
            for e in ENGINES:
                b.wait(e, zs)
        if "xn" in taps:
            tsem = b.sem("tsem")
            b.dma("gpsimd", taps["xn"][:, :], xn[:, :], tsem)
            b.wait("gpsimd", tsem)
        if "g_raw" in taps:
            tsem2 = b.sem("tsem2")
            b.dma("sync", taps["g_raw"][:, :], g_raw[:, :], tsem2)
            b.wait("sync", tsem2)
        if stages in ("all", "lat", "av"):
            s_at = build_l0_attn(b, xn, g_raw, io["x0"], x1, io,
                                 upto=(stages if stages != "all" else "all"))
        else:
            fin = b.sem("fin")
            b.dma("sync", x1[:, :], io["x0"][:, :], fin)
            b.wait("sync", fin)
        b.barrier()
        if nlayers >= 2 and stages == "all":
            build_l1(b, xn, g_raw, scr, x1, x2, io, upto=L1_UPTO)
            if nlayers >= 3:
                build_l2(b, xn, g_raw, x2, x3, io)
            if nlayers >= 4:
                build_l3(b, xn, g_raw, x3, x4, io)
            if "g1" in taps:
                tg = b.sem("tg")
                b.dma("sync", taps["g1"][:, :], g_raw[:, :], tg)
                b.wait("sync", tg)
            if L1_UPTO != "all":
                fin2 = b.sem("fin2")
                b.dma("sync", x2[:, :], x1[:, :], fin2)
                b.wait("sync", fin2)
        b.emit()
    return nc


def kernel(**inputs):
    consts = make_constants()
    nc = build_l0_program(nlayers=NLAYERS)
    in_maps = []
    for core in range(NCORES):
        m = stage_core_inputs(inputs, core)
        for k, v in consts.items():
            m[k] = v
        in_maps.append(m)
    res = run_bass_kernel_spmd(nc, in_maps, core_ids=list(range(NCORES)))
    outs = []
    oname = "x%d" % NLAYERS
    for core in range(NCORES):
        xo = np.asarray(res.results[core][oname])
        outs.append(xo.reshape(D, BLOC, T).transpose(1, 2, 0))
    return np.concatenate(outs, axis=0).astype(np.float32)
